# version 02
# baseline (speedup 1.0000x reference)
import numpy as np
import ml_dtypes
from contextlib import ExitStack

import concourse.bass as bass
import concourse.mybir as mybir
from concourse.bass_utils import run_bass_kernel_spmd

F32 = mybir.dt.float32
BF16 = mybir.dt.bfloat16
U8 = mybir.dt.uint8
AF = mybir.ActivationFunctionType
ALU = mybir.AluOpType
AX = mybir.AxisListType

B, S, D, E, H, Dh = 8, 2048, 1024, 8, 16, 64
NT = S // 128
TFULL = B * S
NFT = TFULL // 128
CAP = 5120
LAT = 256
NEG = -1.0e30


def f32c(x):
    return np.ascontiguousarray(np.asarray(x, dtype=np.float64).astype(np.float32))


class Eng:

    def __init__(self, kb, name):
        self.name = name
        self.sem = kb.ctx.enter_context(kb.nc.semaphore("sm_" + name))
        self.n = 0

    def tag(self, instr, k=1):
        instr.then_inc(self.sem, k)
        self.n += k
        return self.n


class DmaQ:

    def __init__(self, kb, name):
        self.sem = kb.ctx.enter_context(kb.nc.semaphore("dq_" + name))
        self.n = 0

    def start(self, eng, out, in_, **kw):
        ins = eng.dma_start(out=out, in_=in_, **kw)
        ins.then_inc(self.sem, 16)
        self.n += 1
        return 16 * self.n

    @property
    def tok(self):
        return 16 * self.n


class KB:
    def __init__(self, dbg=()):
        self.dbg = set(dbg)
        nc = bass.Bass("TRN2", target_bir_lowering=False, debug=False,
                       detect_race_conditions=False)
        self.nc = nc
        self.ctx = ExitStack()
        self.dbg_outs = {}
        self._dbg_src = {}

        def din(name, shape):
            return nc.dram_tensor(name, list(shape), F32, kind="ExternalInput").ap()

        self.x = din("x", (S, D))
        self.xfull = din("xfull", (TFULL, D))
        self.beforet = din("beforet", (128, NFT))
        w = {}
        for name, shape in [
            ("L0_attn_Wq", (D, D)), ("L0_attn_bq", (D,)),
            ("L0_attn_Wl", (D, LAT)), ("L0_attn_bl", (LAT,)),
            ("L0_attn_Wk", (LAT, D)), ("L0_attn_bk", (D,)),
            ("L0_attn_Wv", (LAT, D)), ("L0_attn_bv", (D,)),
            ("L0_attn_Wo", (D, D)), ("L0_attn_bo", (D,)),
            ("L0_ffn_wr", (E, D)), ("L0_ffn_w1", (E, D, 4096)),
            ("L0_ffn_b1", (E, 4096)), ("L0_ffn_w2", (E, 4096, D)),
            ("L0_ffn_b2", (E, D)),
            ("L1_attn_ln_gamma", (D,)), ("L1_attn_ln_beta", (D,)),
            ("L1_attn_A", (16, 16)), ("L1_attn_C", (16, D)),
            ("L1_attn_w_delta", (D, 16)), ("L1_attn_b_delta", (16,)),
            ("L1_attn_w_in", (D, 16)), ("L1_attn_b_in", (16,)),
            ("L1_attn_w_gate", (D, D)), ("L1_attn_b_gate", (D,)),
            ("L1_ffn_Wqkv", (6, D, 3072)), ("L1_ffn_q_scale", (6, 64)),
            ("L1_ffn_k_scale", (6, 64)), ("L1_ffn_Wo", (6, D, D)),
            ("L1_ffn_ln_g", (6, D)), ("L1_ffn_ln_b", (6, D)),
            ("L1_ffn_W1", (6, D, 5460)), ("L1_ffn_b1", (6, 5460)),
            ("L1_ffn_W2", (6, 2730, D)), ("L1_ffn_b2", (6, D)),
        ]:
            w[name] = din(name, shape)
        self.w = w

        self.y = nc.dram_tensor("y", [S, D], F32, kind="ExternalOutput").ap()

        self.d_x1 = nc.dram_tensor("d_x1", [S, D], F32, kind="Internal").ap()
        self.d_xn1 = nc.dram_tensor("d_xn1", [S, D], F32, kind="Internal").ap()
        self.d_s = nc.dram_tensor("d_s", [S, D], BF16, kind="Internal").ap()
        self.d_yg = nc.dram_tensor("d_yg", [S, D], BF16, kind="Internal").ap()
        self.d_acc = nc.dram_tensor("d_acc", [S, D], F32, kind="Internal").ap()
        self.d_f = nc.dram_tensor("d_f", [S, D], F32, kind="Internal").ap()

        self.ident_f = nc.inline_tensor(np.eye(128, dtype=np.float32), "ident_f").ap()
        self.ident_b = nc.inline_tensor(
            np.eye(128).astype(ml_dtypes.bfloat16), "ident_b").ap()
        self.iota8 = nc.inline_tensor(
            np.tile(np.arange(8, dtype=np.float32), (128, NT)), "iota8").ap()
        half = np.arange(0, Dh, 2, dtype=np.float64)
        inv_freq = 1.0 / 10000.0 ** (half / Dh)
        tt = np.arange(S, dtype=np.float64)
        freqs = tt[:, None] * inv_freq[None, :]
        freqs = np.concatenate([freqs, freqs], -1)
        base_scale = (half + 0.4 * Dh) / (1.4 * Dh)
        power = (tt - S // 2) / (256 // 2)
        scale = base_scale[None, :] ** power[:, None]
        scale = np.concatenate([scale, scale], -1)
        self.rot_cs = nc.inline_tensor(f32c(np.cos(freqs) * scale), "rot_cs").ap()
        self.rot_ss = nc.inline_tensor(f32c(np.sin(freqs) * scale), "rot_ss").ap()
        self.rot_ci = nc.inline_tensor(f32c(np.cos(freqs) / scale), "rot_ci").ap()
        self.rot_si = nc.inline_tensor(f32c(np.sin(freqs) / scale), "rot_si").ap()
        ii = np.arange(128)[:, None]
        jj = np.arange(384)[None, :]
        m_main = np.where((jj >= ii) & (jj <= ii + 256), 0.0, NEG)
        m_a0 = np.where(jj[:, 0:128] <= ii, 0.0, NEG)
        m_a1 = np.where(jj[:, 0:256] <= ii + 128, 0.0, NEG)
        self.mask_main = nc.inline_tensor(f32c(m_main), "mask_main").ap()
        self.mask_a0 = nc.inline_tensor(f32c(m_a0), "mask_a0").ap()
        self.mask_a1 = nc.inline_tensor(f32c(m_a1), "mask_a1").ap()

        self.T = Eng(self, "T")
        self.V = Eng(self, "V")
        self.SC = Eng(self, "S")
        self.G = Eng(self, "G")

        cx = self.ctx
        sb = nc.sbuf_tensor
        self.ident_f_sb = cx.enter_context(sb("ident_f_sb", [128, 128], F32))
        self.ident_b_sb = cx.enter_context(sb("ident_b_sb", [128, 128], BF16))
        self.iota8_sb = cx.enter_context(sb("iota8_sb", [128, NT * 8], F32))
        self.big8_sb = cx.enter_context(sb("big8_sb", [128, NT * 8], F32))
        self.ones_row = cx.enter_context(sb("ones_row", [1, 128], F32))
        self.negmu0 = cx.enter_context(sb("negmu0", [128, NT], F32))
        self.rstd0 = cx.enter_context(sb("rstd0", [128, NT], F32))
        self.c_tm = cx.enter_context(sb("c_tm", [128, NT * 8], F32))
        self.base8 = cx.enter_context(sb("base8", [8, 1], F32))
        self.epsb = cx.enter_context(sb("epsb", [128, 1], F32))
        self.beforet_sb = cx.enter_context(sb("beforet_sb", [128, NFT], F32))
        self.b0ctx = ExitStack()
        self.xnT = self.b0ctx.enter_context(sb("xnT", [128, 8 * S], BF16))

        self.block = cx.enter_context(nc.Block())

        self.n_dq = 0
        self.dY = self.new_dq("Y")
        self.dW = self.new_dq("W")

        def _(sync):
            self.dY.start(sync, self.ident_f_sb[:], self.ident_f)
            self.dY.start(sync, self.ident_b_sb[:], self.ident_b)
            self.dY.start(sync, self.iota8_sb[:], self.iota8)
            self.dY.start(sync, self.beforet_sb[:], self.beforet)
        self.block.sync(_)
        self.const_tok = self.dY.tok

        def _(v):
            v.memset(self.big8_sb[:], 100.0)
            v.memset(self.epsb[:], 1e-5)
            self.V.tag(v.memset(self.ones_row[:], 1.0), 3)
        self.block.vector(_)

    def new_dq(self, name):
        pool = getattr(self, "_dq_pool", None)
        if pool is None:
            pool = self._dq_pool = {}
        if name in pool:
            return pool[name]
        self.n_dq = getattr(self, "n_dq", 0) + 1
        assert self.n_dq < 85, "running out of semaphores for DMA queues"
        q = DmaQ(self, f"{name}{self.n_dq}")
        pool[name] = q
        return q

    def on(self, engname, fn):
        getattr(self.block, engname)(fn)

    def bar(self):
        return dict(T=self.T.n, V=self.V.n, S=self.SC.n, G=self.G.n,
                    dY=self.dY.tok, dW=self.dW.tok)

    def wait_bar(self, e, barr, skip=()):
        for nm, eng in [("T", self.T), ("V", self.V), ("S", self.SC),
                        ("G", self.G)]:
            if nm in skip or barr[nm] == 0:
                continue
            e.wait_ge(eng.sem, barr[nm])
        if "dY" not in skip and barr["dY"]:
            e.wait_ge(self.dY.sem, barr["dY"])
        if "dW" not in skip and barr["dW"]:
            e.wait_ge(self.dW.sem, barr["dW"])

    def dbg_dump(self, name, src_ap, shape, dtype=F32):
        if name not in self.dbg:
            return
        ap = self.nc.dram_tensor("dbg_" + name, list(shape), dtype,
                                 kind="ExternalOutput").ap()
        self.dbg_outs[name] = ap
        kb = self

        def _(sync):
            sync.wait_ge(kb.T.sem, kb.T.n) if kb.T.n else None
            sync.wait_ge(kb.V.sem, kb.V.n) if kb.V.n else None
            sync.wait_ge(kb.SC.sem, kb.SC.n) if kb.SC.n else None
            kb.dY.start(sync, ap, src_ap)
        self.on("sync", _)
        self._dbg_src[name] = None

    def dbg_tap(self, name, src_ap, shape, dtype=F32):
        if name not in self.dbg:
            return
        ap = self.nc.dram_tensor("dbg_" + name, list(shape), dtype,
                                 kind="ExternalOutput").ap()
        self.dbg_outs[name] = ap
        self._dbg_src[name] = src_ap

    def p0(self):
        kb, nc = self, self.nc
        pctx = ExitStack()
        sbt = nc.sbuf_tensor
        xbuf = [pctx.enter_context(sbt(f"p0x{p}", [128, D], F32)) for p in range(2)]
        xn_tm = [pctx.enter_context(sbt(f"p0n{p}", [128, D], BF16)) for p in range(2)]
        sq = pctx.enter_context(sbt("p0sq", [128, D], F32))
        std = pctx.enter_context(sbt("p0std", [128, 1], F32))
        ptr = pctx.enter_context(nc.psum_tensor("p0tr", [128, D], BF16))
        ld = [DmaQ(kb, f"p0f{p}") for p in range(2)]
        vfree = [0, 0]
        tfree = [0, 0]
        evac_tok = [0]

        for i in range(NT):
            p = i % 2

            def _(sync, i=i, p=p):
                if vfree[p]:
                    sync.wait_ge(kb.V.sem, vfree[p])
                ld[p].start(sync, xbuf[p][:], kb.x[i * 128:(i + 1) * 128, :])
            self.on("sync", _)

            def _(v, i=i, p=p):
                v.wait_ge(ld[p].sem, ld[p].tok)
                t0 = kb.V.tag(v.tensor_reduce(
                    kb.negmu0[:, i:i + 1], xbuf[p][:], axis=AX.X, op=ALU.add,
                    negate=True))
                v.wait_ge(kb.V.sem, t0)
                kb.V.tag(v.tensor_scalar_mul(
                    kb.negmu0[:, i:i + 1], kb.negmu0[:, i:i + 1], 1.0 / D))
            self.on("vector", _)

            def _(s, i=i, p=p):
                s.wait_ge(kb.V.sem, kb.V.n)
                kb.SC.tag(s.activation(sq[:], xbuf[p][:], AF.Square,
                                       bias=kb.negmu0[:, i:i + 1]))
            self.on("scalar", _)

            def _(v, i=i):
                v.wait_ge(kb.SC.sem, kb.SC.n)
                kb.V.tag(v.tensor_reduce(std[:], sq[:], axis=AX.X, op=ALU.add))
            self.on("vector", _)

            def _(s, i=i):
                s.wait_ge(kb.V.sem, kb.V.n)
                kb.SC.tag(s.activation(std[:], std[:], AF.Sqrt,
                                       scale=1.0 / D, bias=kb.epsb[:, 0:1]))
            self.on("scalar", _)

            def _(v, i=i, p=p):
                v.wait_ge(kb.SC.sem, kb.SC.n)
                t0 = kb.V.tag(v.reciprocal(kb.rstd0[:, i:i + 1], std[:]))
                v.wait_ge(kb.V.sem, t0)
                if tfree[p]:
                    v.wait_ge(kb.T.sem, tfree[p])
                vfree[p] = kb.V.tag(v.tensor_scalar(
                    xn_tm[p][:], xbuf[p][:], kb.negmu0[:, i:i + 1],
                    kb.rstd0[:, i:i + 1], op0=ALU.add, op1=ALU.mult))
            self.on("vector", _)

            def _(t, i=i, p=p):
                t.wait_ge(kb.V.sem, vfree[p])
                if i >= 1:
                    t.wait_ge(kb.V.sem, evac_tok[0])
                ins = None
                for c in range(8):
                    ins = t.transpose(ptr[:, c * 128:(c + 1) * 128],
                                      xn_tm[p][:, c * 128:(c + 1) * 128],
                                      kb.ident_b_sb[:])
                tfree[p] = kb.T.tag(ins)
            self.on("tensor", _)

            def _(v, i=i, p=p):
                v.wait_ge(kb.T.sem, tfree[p])
                evac_tok[0] = kb.V.tag(v.tensor_copy(
                    kb.xnT[:].rearrange("q (c t) -> q c t", c=8)
                    [:, :, i * 128:(i + 1) * 128],
                    ptr[:].rearrange("q (c t) -> q c t", c=8)))
            self.on("vector", _)
        self.dbg_tap("xnT", kb.xnT[:], [128, 8 * S], BF16)
        self.dbg_tap("rstd0", kb.rstd0[:], [128, NT])
        pctx.close()

    def p1(self):
        kb, nc = self, self.nc
        p0bar = self.bar()
        pctx = ExitStack()
        sbt = nc.sbuf_tensor
        wr_sb = pctx.enter_context(sbt("wr_sb", [8, D], F32))
        kb.pos8 = pctx.enter_context(sbt("pos8", [8, S], F32))
        kb.zero8 = pctx.enter_context(sbt("zero8", [8, S], F32))
        wrT = pctx.enter_context(sbt("wrT", [128, 8 * 8], F32))
        srow = pctx.enter_context(sbt("srow", [1, 8], F32))
        scol = pctx.enter_context(sbt("scol", [8, 1], F32))

        GB = 8
        xf = [pctx.enter_context(sbt(f"p1x{p}", [128, D], F32)) for p in range(2)]
        xT = pctx.enter_context(sbt("p1xT", [128, D], F32))
        lg_tm = pctx.enter_context(sbt("p1lg", [128, GB * 8], F32))
        negmu = pctx.enter_context(sbt("p1negmu", [128, GB], F32))
        nmrow = pctx.enter_context(sbt("p1nmrow", [1, 128 * GB], F32))
        dots_sb = pctx.enter_context(sbt("p1dots", [8, 128], F32))
        m1 = pctx.enter_context(sbt("p1m1", [128, GB], F32))
        m2 = pctx.enter_context(sbt("p1m2", [128, GB], F32))
        i1b = pctx.enter_context(sbt("p1i1", [128, GB], F32))
        i2b = pctx.enter_context(sbt("p1i2", [128, GB], F32))
        equ = pctx.enter_context(sbt("p1eq", [128, GB * 8], U8))
        cand = pctx.enter_context(sbt("p1cand", [128, GB * 8], F32))
        oh1 = pctx.enter_context(sbt("p1oh1", [128, GB * 8], F32))
        oh2 = pctx.enter_context(sbt("p1oh2", [128, GB * 8], F32))
        lg2 = pctx.enter_context(sbt("p1lg2", [128, GB * 8], F32))
        mm_tm = pctx.enter_context(sbt("p1mm", [128, GB * 8], F32))
        gmat_own = pctx.enter_context(sbt("gmat_own", [8, S], F32))
        maskt_own = pctx.enter_context(sbt("maskt_own", [8, S], F32))

        p_xtr = pctx.enter_context(nc.psum_tensor("p1ptr", [128, D], F32))
        p_dots = pctx.enter_context(nc.psum_tensor("p1pd", [8, 128], F32))
        p_lgt = pctx.enter_context(nc.psum_tensor("p1plg", [128, 8], F32))
        p_nmr = pctx.enter_context(nc.psum_tensor("p1pnm", [1, 128], F32))
        p_msk = pctx.enter_context(nc.psum_tensor("p1pmsk", [8, 128 * GB], F32))
        p_base = pctx.enter_context(nc.psum_tensor("p1pbase", [8, 1], F32))
        ld = [DmaQ(kb, f"p1l{p}") for p in range(2)]

        st = dict(vdone=[0, 0], xT_evac=0, msk_tr=0, lgt_evac=0)

        def _(v):
            self.wait_bar(v, p0bar, skip=("V", "dW", "dY"))
            kb.V.tag(v.memset(kb.zero8[:], 0.0))
        self.on("vector", _)

        wr_dq = self.new_dq("wr")

        def _(sync):
            self.wait_bar(sync, p0bar, skip=("dY", "dW"))
            wr_dq.start(sync, wr_sb[:], kb.w["L0_ffn_wr"])
        self.on("sync", _)

        def _(t):
            t.wait_ge(wr_dq.sem, wr_dq.tok)
            self.wait_bar(t, p0bar, skip=("T", "dY", "dW"))
            ins = None
            for c in range(8):
                ins = t.transpose(p_xtr[0:128, c * 8:(c + 1) * 8],
                                  wr_sb[:, c * 128:(c + 1) * 128],
                                  kb.ident_f_sb[0:8, 0:8])
            kb.T.tag(ins)
        self.on("tensor", _)

        def _(v):
            v.wait_ge(kb.T.sem, kb.T.n)
            kb.V.tag(v.tensor_copy(wrT[:], p_xtr[0:128, 0:64]))
            kb.V.tag(v.tensor_reduce(scol[:], wr_sb[:], axis=AX.X, op=ALU.add))
        self.on("vector", _)

        def _(t):
            t.wait_ge(kb.V.sem, kb.V.n)
            kb.T.tag(t.transpose(p_nmr[0:1, 0:8], scol[:],
                                 kb.ident_f_sb[0:8, 0:8]))
        self.on("tensor", _)

        def _(v):
            v.wait_ge(kb.T.sem, kb.T.n)
            kb.V.tag(v.tensor_copy(srow[:], p_nmr[0:1, 0:8]))
        self.on("vector", _)
        srow_tok = kb.V.n

        def top2_batch(ng, i0_abs, own):
            n8 = ng * 8

            def g3(ap):
                return ap[:, 0:n8].rearrange("p (g e) -> p g e", g=ng)

            def scb(ap):
                return ap[:, 0:ng].unsqueeze(2).broadcast_to([128, ng, 8])

            def _(v):
                v.wait_ge(kb.V.sem, kb.V.n)
                if st["msk_tr"]:
                    v.wait_ge(kb.T.sem, st["msk_tr"])
                t = kb.V.tag(v.tensor_reduce(m1[:, 0:ng], g3(lg_tm),
                                             axis=AX.X, op=ALU.max))
                v.wait_ge(kb.V.sem, t)
                t = kb.V.tag(v.tensor_tensor(g3(equ), g3(lg_tm), scb(m1),
                                             op=ALU.is_ge))
                v.wait_ge(kb.V.sem, t)
                t = kb.V.tag(v.select(cand[:, 0:n8], equ[:, 0:n8],
                                      kb.iota8_sb[:, 0:n8], kb.big8_sb[:, 0:n8]))
                v.wait_ge(kb.V.sem, t)
                t = kb.V.tag(v.tensor_reduce(i1b[:, 0:ng], g3(cand),
                                             axis=AX.X, op=ALU.min))
                v.wait_ge(kb.V.sem, t)
                t = kb.V.tag(v.tensor_tensor(g3(oh1), g3(kb.iota8_sb), scb(i1b),
                                             op=ALU.is_equal))
                v.wait_ge(kb.V.sem, t)
                t = kb.V.tag(v.scalar_tensor_tensor(
                    lg2[:, 0:n8], oh1[:, 0:n8], NEG, lg_tm[:, 0:n8],
                    op0=ALU.mult, op1=ALU.add))
                v.wait_ge(kb.V.sem, t)
                t = kb.V.tag(v.tensor_reduce(m2[:, 0:ng], g3(lg2),
                                             axis=AX.X, op=ALU.max))
                v.wait_ge(kb.V.sem, t)
                t = kb.V.tag(v.tensor_tensor(g3(equ), g3(lg2), scb(m2),
                                             op=ALU.is_ge))
                v.wait_ge(kb.V.sem, t)
                t = kb.V.tag(v.select(cand[:, 0:n8], equ[:, 0:n8],
                                      kb.iota8_sb[:, 0:n8], kb.big8_sb[:, 0:n8]))
                v.wait_ge(kb.V.sem, t)
                t = kb.V.tag(v.tensor_reduce(i2b[:, 0:ng], g3(cand),
                                             axis=AX.X, op=ALU.min))
                v.wait_ge(kb.V.sem, t)
                t = kb.V.tag(v.tensor_tensor(g3(oh2), g3(kb.iota8_sb), scb(i2b),
                                             op=ALU.is_equal))
                v.wait_ge(kb.V.sem, t)
                kb.V.tag(v.tensor_add(mm_tm[:, 0:n8], oh1[:, 0:n8],
                                      oh2[:, 0:n8]))
            self.on("vector", _)

            if own:
                def _(v):
                    t = kb.V.tag(v.tensor_tensor(m1[:, 0:ng], m1[:, 0:ng],
                                                 m2[:, 0:ng], op=ALU.subtract))
                    v.wait_ge(kb.V.sem, t)
                    kb.V.tag(v.tensor_mul(m1[:, 0:ng], m1[:, 0:ng],
                                          kb.rstd0[:, i0_abs:i0_abs + ng]))
                self.on("vector", _)

                def _(s):
                    s.wait_ge(kb.V.sem, kb.V.n)
                    t = kb.SC.tag(s.activation(m2[:, 0:ng], m1[:, 0:ng],
                                               AF.Sigmoid))
                    s.wait_ge(kb.SC.sem, t)
                    kb.SC.tag(s.activation(i1b[:, 0:ng], m1[:, 0:ng],
                                           AF.Sigmoid, scale=-1.0))
                self.on("scalar", _)

                def _(v):
                    v.wait_ge(kb.SC.sem, kb.SC.n)
                    t = kb.V.tag(v.tensor_mul(g3(oh1), g3(oh1), scb(m2)))
                    v.wait_ge(kb.V.sem, t)
                    t = kb.V.tag(v.tensor_mul(g3(oh2), g3(oh2), scb(i1b)))
                    v.wait_ge(kb.V.sem, t)
                    kb.V.tag(v.tensor_add(oh1[:, 0:n8], oh1[:, 0:n8],
                                          oh2[:, 0:n8]))
                self.on("vector", _)

            if not own:
                def _(t):
                    t.wait_ge(kb.V.sem, kb.V.n)
                    t.wait_ge(kb.dY.sem, self.const_tok)
                    for j in range(ng):
                        gi = i0_abs + j
                        ins = t.matmul(p_base[0:8, 0:1],
                                       mm_tm[:, j * 8:(j + 1) * 8],
                                       kb.beforet_sb[:, gi:gi + 1],
                                       start=(gi == 0), stop=(gi == NFT - 1))
                    st["msk_tr"] = kb.T.tag(ins)
                self.on("tensor", _)
            else:
                def _(t):
                    t.wait_ge(kb.V.sem, kb.V.n)
                    ins = None
                    for j in range(ng):
                        ins = t.transpose(p_msk[0:8, j * 128:(j + 1) * 128],
                                          mm_tm[:, j * 8:(j + 1) * 8],
                                          kb.ident_f_sb[:])
                    kb.T.tag(ins)
                self.on("tensor", _)

                def _(v):
                    v.wait_ge(kb.T.sem, kb.T.n)
                    kb.V.tag(v.tensor_copy(
                        maskt_own[0:8, i0_abs * 128:(i0_abs + ng) * 128],
                        p_msk[0:8, 0:n8 * 16]))
                self.on("vector", _)

                def _(t):
                    t.wait_ge(kb.V.sem, kb.V.n)
                    ins = None
                    for j in range(ng):
                        ins = t.transpose(p_msk[0:8, j * 128:(j + 1) * 128],
                                          oh1[:, j * 8:(j + 1) * 8],
                                          kb.ident_f_sb[:])
                    st["msk_tr"] = kb.T.tag(ins)
                self.on("tensor", _)

                def _(v):
                    v.wait_ge(kb.T.sem, kb.T.n)
                    kb.V.tag(v.tensor_copy(
                        gmat_own[0:8, i0_abs * 128:(i0_abs + ng) * 128],
                        p_msk[0:8, 0:n8 * 16]))
                self.on("vector", _)

        def routing_tiles(src_ap, n_tiles, own):
            for i in range(n_tiles):
                p = i % 2
                g = i % GB

                def _(sync, i=i, p=p):
                    if st["vdone"][p]:
                        sync.wait_ge(kb.V.sem, st["vdone"][p])
                    else:
                        self.wait_bar(sync, p0bar, skip=("dY", "dW"))
                    ld[p].start(sync, xf[p][:], src_ap[i * 128:(i + 1) * 128, :])
                self.on("sync", _)

                def _(v, i=i, p=p, g=g):
                    v.wait_ge(ld[p].sem, ld[p].tok)
                    t0 = kb.V.tag(v.tensor_reduce(
                        negmu[:, g:g + 1], xf[p][:], axis=AX.X, op=ALU.add,
                        negate=True))
                    v.wait_ge(kb.V.sem, t0)
                    kb.V.tag(v.tensor_scalar_mul(
                        negmu[:, g:g + 1], negmu[:, g:g + 1], 1.0 / D))
                self.on("vector", _)
                nm_tok = kb.V.n

                def _(t, i=i, p=p, g=g):
                    t.wait_ge(kb.V.sem, nm_tok)
                    if st["xT_evac"]:
                        t.wait_ge(kb.V.sem, st["xT_evac"])
                    ins = None
                    for c in range(8):
                        ins = t.transpose(p_xtr[:, c * 128:(c + 1) * 128],
                                          xf[p][:, c * 128:(c + 1) * 128],
                                          kb.ident_f_sb[:])
                    kb.T.tag(ins)
                    kb.T.tag(t.transpose(p_nmr[0:1, :], negmu[:, g:g + 1],
                                         kb.ident_f_sb[:]))
                self.on("tensor", _)
                tr_tok = kb.T.n

                def _(v, i=i, p=p, g=g):
                    v.wait_ge(kb.T.sem, tr_tok)
                    st["vdone"][p] = kb.V.tag(v.tensor_copy(xT[:], p_xtr[:]))
                    st["xT_evac"] = st["vdone"][p]
                    kb.V.tag(v.tensor_copy(nmrow[0:1, g * 128:(g + 1) * 128],
                                           p_nmr[0:1, :]))
                self.on("vector", _)
                ev_tok = kb.V.n

                def _(t, i=i, g=g):
                    t.wait_ge(kb.V.sem, ev_tok)
                    for c in range(8):
                        t.matmul(p_dots[:], wrT[:, c * 8:(c + 1) * 8],
                                 xT[:, c * 128:(c + 1) * 128],
                                 start=(c == 0), stop=False)
                    kb.T.tag(t.matmul(p_dots[:], srow[:],
                                      nmrow[0:1, g * 128:(g + 1) * 128],
                                      start=False, stop=True))
                self.on("tensor", _)

                def _(v, i=i):
                    v.wait_ge(kb.T.sem, kb.T.n)
                    kb.V.tag(v.tensor_copy(dots_sb[:], p_dots[:]))
                self.on("vector", _)

                def _(t, i=i):
                    t.wait_ge(kb.V.sem, kb.V.n)
                    if st["lgt_evac"]:
                        t.wait_ge(kb.V.sem, st["lgt_evac"])
                    kb.T.tag(t.transpose(p_lgt[:, 0:8], dots_sb[:],
                                         kb.ident_f_sb[0:8, 0:8]))
                self.on("tensor", _)

                def _(v, i=i, g=g):
                    v.wait_ge(kb.T.sem, kb.T.n)
                    st["lgt_evac"] = kb.V.tag(v.tensor_copy(
                        lg_tm[:, g * 8:(g + 1) * 8], p_lgt[:, 0:8]))
                self.on("vector", _)

                if g == GB - 1 or i == n_tiles - 1:
                    top2_batch(g + 1, i - g, own)

        routing_tiles(self.xfull, NFT, own=False)

        def _(v):
            v.wait_ge(kb.T.sem, kb.T.n)
            kb.V.tag(v.tensor_copy(kb.base8[:], p_base[0:8, 0:1]))
        self.on("vector", _)

        routing_tiles(self.x, NT, own=True)

        def _(v):
            v.wait_ge(kb.V.sem, kb.V.n)
            t1 = kb.V.tag(v.tensor_tensor_scan(
                kb.pos8[0:8, 0:S], maskt_own[:], kb.zero8[:], 0.0,
                op0=ALU.add, op1=ALU.add))
            v.wait_ge(kb.V.sem, t1)
            t2 = kb.V.tag(v.tensor_scalar(
                kb.pos8[0:8, 0:S], kb.pos8[0:8, 0:S], kb.base8[0:8, 0:1], None,
                op0=ALU.add))
            v.wait_ge(kb.V.sem, t2)
            t3 = kb.V.tag(v.tensor_single_scalar(
                kb.pos8[0:8, 0:S], kb.pos8[0:8, 0:S], CAP + 0.5, op=ALU.is_lt))
            v.wait_ge(kb.V.sem, t3)
            t4 = kb.V.tag(v.tensor_mul(gmat_own[:], gmat_own[:],
                                       kb.pos8[0:8, 0:S]))
            v.wait_ge(kb.V.sem, t4)
        self.on("vector", _)

        evt = [0]
        for i in range(NT):
            def _(t, i=i):
                t.wait_ge(kb.V.sem, kb.V.n if i == 0 else evt[0])
                kb.T.tag(t.transpose(p_lgt[:, 0:8],
                                     gmat_own[0:8, i * 128:(i + 1) * 128],
                                     kb.ident_f_sb[0:8, 0:8]))
            self.on("tensor", _)

            def _(v, i=i):
                v.wait_ge(kb.T.sem, kb.T.n)
                evt[0] = kb.V.tag(v.tensor_copy(kb.c_tm[:, i * 8:(i + 1) * 8],
                                                p_lgt[:, 0:8]))
            self.on("vector", _)

        self.dbg_tap("c_tm", kb.c_tm[:], [128, NT * 8])
        self.dbg_tap("base8", kb.base8[:], [8, 1])
        self.dbg_tap("gmat_own", gmat_own[:], [8, S])
        if "gmat_own" in self.dbg or "maskt_own" in self.dbg:
            pctx.pop_all()
        if "maskt_own" in self.dbg:
            self.dbg_tap("maskt_own", maskt_own[:], [8, S])
        if "gmat_own" not in self.dbg and "maskt_own" not in self.dbg:
            pctx.close()


    def load_w_lhsT(self, pctx, name, w_ap, kin, kout, bar=None):
        nc = self.nc
        nic = kin // 128
        slab = pctx.enter_context(nc.sbuf_tensor(name, [128, nic * kout], BF16))
        import re as _re
        dq = self.new_dq("w_" + _re.sub(r"^l\d+", "L", name))

        if bar is None:
            bar = self.bar()

        def _(g):
            self.wait_bar(g, bar, skip=("G", "dW"))
            for ic in range(nic):
                dq.start(g, slab[:, ic * kout:(ic + 1) * kout],
                         w_ap[ic * 128:(ic + 1) * 128, :])
        self.on("gpsimd", _)
        return slab, dq

    def load_row(self, pctx, name, b_ap, n, bar=None):
        nc = self.nc
        row = pctx.enter_context(nc.sbuf_tensor(name, [1, n], F32))
        import re as _re
        dq = self.new_dq("r_" + _re.sub(r"^l\d+", "L", name))

        if bar is None:
            bar = self.bar()

        def _(sync):
            self.wait_bar(sync, bar, skip=("dW",))
            dq.start(sync, row[0:1, :], b_ap.unsqueeze(0) if b_ap.ndim == 1 else b_ap)
        self.on("sync", _)
        return row, dq

    def row_to_cols(self, pctx, name, row, n, row_dq, psum):
        nc = self.nc
        ncol = n // 128
        cols = pctx.enter_context(nc.sbuf_tensor(name, [128, ncol], F32))

        def _(t):
            t.wait_ge(row_dq.sem, row_dq.tok)
            t.wait_ge(self.V.sem, self.V.n)
            ins = None
            for c in range(ncol):
                ins = t.transpose(psum[:, c:c + 1],
                                  row[0:1, c * 128:(c + 1) * 128],
                                  self.ident_f_sb[0:1, 0:1])
            self.T.tag(ins)
        self.on("tensor", _)

        def _(v):
            v.wait_ge(self.T.sem, self.T.n)
            self.V.tag(v.tensor_copy(cols[:], psum[:, 0:ncol]))
        self.on("vector", _)
        return cols

    def p2(self):
        kb, nc = self, self.nc
        bar = self.bar()
        pctx = ExitStack()
        sbt = nc.sbuf_tensor

        oT = pctx.enter_context(sbt("oT", [128, 8 * S], BF16))
        onescol_b = pctx.enter_context(sbt("onescol_b", [128, 1], BF16))
        qctx = ExitStack()
        q_fm = qctx.enter_context(sbt("q_fm", [128, 8 * S], BF16))
        k_fm = qctx.enter_context(sbt("k_fm", [128, 8 * S], BF16))
        v_tm = qctx.enter_context(sbt("v_tm", [128, 16 * D], BF16))
        latctx = ExitStack()
        lat_fm = latctx.enter_context(sbt("lat_fm", [128, 2 * S], BF16))

        mctx = ExitStack()
        p_mm = mctx.enter_context(nc.psum_tensor("p2mm", [128, 512], F32))
        p_mm2 = mctx.enter_context(nc.psum_tensor("p2mm2", [128, 512], F32))
        p_bc = mctx.enter_context(nc.psum_tensor("p2bc", [128, 32], F32))

        def _(v):
            self.wait_bar(v, bar, skip=("V", "dW"))
            kb.V.tag(v.memset(onescol_b[:], 1.0))
        self.on("vector", _)

        w1ctx = ExitStack()
        wq, wq_dq = self.load_w_lhsT(w1ctx, "wq", kb.w["L0_attn_Wq"], D, D, bar)
        wl, wl_dq = self.load_w_lhsT(w1ctx, "wl", kb.w["L0_attn_Wl"], D, LAT)
        bq_row, bq_dq = self.load_row(w1ctx, "bq_row", kb.w["L0_attn_bq"], D, bar)
        bl_row, bl_dq = self.load_row(w1ctx, "bl_row", kb.w["L0_attn_bl"], LAT)
        bq_col = self.row_to_cols(w1ctx, "bq_col", bq_row, D, bq_dq, p_bc)
        bl_col = self.row_to_cols(w1ctx, "bl_col", bl_row, LAT, bl_dq, p_bc)

        def proj_fm(dst, wslab, w_dq, kin, kout, bias_col, src=None):
            src = src if src is not None else kb.xnT
            nic = kin // 128
            noc = kout // 128
            for oc in range(noc):
                for tc in range(4):
                    def _(t, oc=oc, tc=tc):
                        t.wait_ge(w_dq.sem, w_dq.tok)
                        t.wait_ge(kb.V.sem, kb.V.n)
                        pm = p_mm if (oc * 4 + tc) % 2 == 0 else p_mm2
                        ins = None
                        for ic in range(nic):
                            ins = t.matmul(
                                pm[:, :],
                                wslab[:, ic * kout + oc * 128:ic * kout + (oc + 1) * 128],
                                src[:, ic * S + tc * 512:ic * S + (tc + 1) * 512],
                                start=(ic == 0), stop=(ic == nic - 1))
                        kb.T.tag(ins)
                    self.on("tensor", _)

                    def _(v, oc=oc, tc=tc):
                        v.wait_ge(kb.T.sem, kb.T.n)
                        pm = p_mm if (oc * 4 + tc) % 2 == 0 else p_mm2
                        kb.V.tag(v.tensor_scalar(
                            dst[:, oc * S + tc * 512:oc * S + (tc + 1) * 512],
                            pm[:, :], bias_col[:, oc:oc + 1], None, op0=ALU.add))
                    self.on("vector", _)

        proj_fm(q_fm, wq, wq_dq, D, D, bq_col)
        self.dbg_dump("wq_slab", wq[:], [128, 8 * D], BF16)
        self.dbg_dump("bq_col", bq_col[:], [128, 8])
        proj_fm(lat_fm, wl, wl_dq, D, LAT, bl_col)
        w1ctx.close()

        w2ctx = ExitStack()
        wk, wk_dq = self.load_w_lhsT(w2ctx, "wk", kb.w["L0_attn_Wk"], LAT, D)
        wv, wv_dq = self.load_w_lhsT(w2ctx, "wv", kb.w["L0_attn_Wv"], LAT, D)
        bk_row, bk_dq = self.load_row(w2ctx, "bk_row", kb.w["L0_attn_bk"], D)
        bv_row, bv_dq = self.load_row(w2ctx, "bv_row", kb.w["L0_attn_bv"], D)
        bk_col = self.row_to_cols(w2ctx, "bk_col", bk_row, D, bk_dq, p_bc)

        proj_fm(k_fm, wk, wk_dq, LAT, D, bk_col, src=lat_fm)

        for ti in range(NT):
            for dc in range(2):
                def _(t, ti=ti, dc=dc):
                    t.wait_ge(wv_dq.sem, wv_dq.tok)
                    t.wait_ge(bv_dq.sem, bv_dq.tok)
                    t.wait_ge(kb.V.sem, kb.V.n)
                    pm = p_mm if dc == 0 else p_mm2
                    for lc in range(2):
                        t.matmul(pm[:, :],
                                 lat_fm[:, lc * S + ti * 128:lc * S + (ti + 1) * 128],
                                 wv[:, lc * D + dc * 512:lc * D + (dc + 1) * 512],
                                 start=(lc == 0), stop=False)
                    ins = t.matmul(pm[:, :], kb.ones_row[0:1, 0:128],
                                   bv_row[0:1, dc * 512:(dc + 1) * 512],
                                   start=False, stop=True)
                    kb.T.tag(ins)
                self.on("tensor", _)

                def _(v, ti=ti, dc=dc):
                    v.wait_ge(kb.T.sem, kb.T.n)
                    pm = p_mm if dc == 0 else p_mm2
                    kb.V.tag(v.tensor_copy(
                        v_tm[:, ti * D + dc * 512:ti * D + (dc + 1) * 512],
                        pm[:, :]))
                self.on("vector", _)

        self.dbg_dump("q_fm", q_fm[:], [128, 8 * S], BF16)
        self.dbg_dump("k_fm", k_fm[:], [128, 8 * S], BF16)
        self.dbg_dump("v_tm", v_tm[:], [128, 16 * D], BF16)
        self.dbg_dump("lat_fm", lat_fm[:], [128, 2 * S], BF16)

        def _(t):
            t.wait_ge(kb.V.sem, kb.V.n)
        self.on("tensor", _)
        mctx.close()
        w2ctx.close()
        latctx.close()
        actx = ExitStack()
        PT = actx.enter_context(sbt("p2PT", [128, 16 * 512], BF16))
        rb_sb = actx.enter_context(sbt("p2rb", [128, 512], F32))
        nb_row = actx.enter_context(sbt("p2nb", [1, 512], F32))
        den_r = actx.enter_context(sbt("p2den", [1, 512], F32))
        rmax = actx.enter_context(sbt("p2rmax", [128, 1], F32))
        p_S = actx.enter_context(nc.psum_tensor("p2pS", [128, 2048], F32))
        p_den = actx.enter_context(nc.psum_tensor("p2pden", [1, 512], F32))
        p_rb = actx.enter_context(nc.psum_tensor("p2prb", [128, 512], F32))
        p_O = actx.enter_context(nc.psum_tensor("p2pO", [128, 512], F32))
        p_nbt = actx.enter_context(nc.psum_tensor("p2pnbt", [1, 128], F32))

        prev = dict(exp_done=0, pv_done=0, o_evac=0, rb_evac=0, nb_evac=0,
                    den_evac=0)
        for h in range(H):
            hc, hr = h // 2, (h % 2) * 64
            for grp in range(4):
                for qt in range(4):
                    def _(t, h=h, grp=grp, qt=qt, hc=hc, hr=hr):
                        t.wait_ge(kb.V.sem, kb.V.n)
                        if prev["exp_done"]:
                            t.wait_ge(kb.SC.sem, prev["exp_done"])
                        ins = None
                        for jc in range(4):
                            ins = t.matmul(
                                p_S[:, jc * 512:(jc + 1) * 512],
                                q_fm[hr:hr + 64,
                                     hc * S + grp * 512 + qt * 128:
                                     hc * S + grp * 512 + (qt + 1) * 128],
                                k_fm[hr:hr + 64, hc * S + jc * 512:hc * S + (jc + 1) * 512],
                                start=True, stop=True)
                        kb.T.tag(ins)
                    self.on("tensor", _)

                    def _(v, qt=qt):
                        v.wait_ge(kb.T.sem, kb.T.n)
                        t0 = kb.V.tag(v.tensor_reduce(
                            rmax[:], p_S[:], axis=AX.X, op=ALU.max, negate=True))
                        v.wait_ge(kb.V.sem, t0)
                    self.on("vector", _)

                    def _(t, qt=qt):
                        t.wait_ge(kb.V.sem, kb.V.n)
                        if prev["nb_evac"]:
                            t.wait_ge(kb.V.sem, prev["nb_evac"])
                        kb.T.tag(t.transpose(p_nbt[0:1, :], rmax[:],
                                             kb.ident_f_sb[:]))
                    self.on("tensor", _)

                    def _(v, qt=qt):
                        v.wait_ge(kb.T.sem, kb.T.n)
                        prev["nb_evac"] = kb.V.tag(v.tensor_copy(
                            nb_row[0:1, qt * 128:(qt + 1) * 128], p_nbt[0:1, :]))
                    self.on("vector", _)

                for ktg in range(4):
                    def _(t, h=h, grp=grp, ktg=ktg, hc=hc, hr=hr):
                        t.wait_ge(kb.V.sem, kb.V.n)
                        if prev["exp_done"]:
                            t.wait_ge(kb.SC.sem, prev["exp_done"])
                        ins = None
                        for k4 in range(4):
                            kt = ktg * 4 + k4
                            t.matmul(p_S[:, k4 * 512:(k4 + 1) * 512],
                                     k_fm[hr:hr + 64, hc * S + kt * 128:hc * S + (kt + 1) * 128],
                                     q_fm[hr:hr + 64, hc * S + grp * 512:hc * S + grp * 512 + 512],
                                     start=True, stop=False)
                            ins = t.matmul(p_S[:, k4 * 512:(k4 + 1) * 512],
                                           kb.ones_row[0:1, 0:128],
                                           nb_row[0:1, 0:512],
                                           start=False, stop=True)
                        kb.T.tag(ins)
                    self.on("tensor", _)

                    def _(sc, ktg=ktg):
                        sc.wait_ge(kb.T.sem, kb.T.n)
                        if prev["pv_done"]:
                            sc.wait_ge(kb.T.sem, prev["pv_done"])
                        prev["exp_done"] = kb.SC.tag(sc.activation(
                            PT[:, ktg * 2048:(ktg + 1) * 2048], p_S[:],
                            AF.Exp, scale=0.125))
                    self.on("scalar", _)

                def _(t):
                    t.wait_ge(kb.SC.sem, kb.SC.n)
                    if prev["den_evac"]:
                        t.wait_ge(kb.V.sem, prev["den_evac"])
                    ins = None
                    for kt in range(16):
                        ins = t.matmul(p_den[0:1, :], onescol_b[:],
                                       PT[:, kt * 512:(kt + 1) * 512],
                                       start=(kt == 0), stop=(kt == 15))
                    kb.T.tag(ins)
                self.on("tensor", _)

                def _(v):
                    v.wait_ge(kb.T.sem, kb.T.n)
                    t0 = kb.V.tag(v.reciprocal(den_r[0:1, :], p_den[0:1, :]))
                    prev["den_evac"] = t0
                    v.wait_ge(kb.V.sem, t0)
                self.on("vector", _)

                def _(t):
                    t.wait_ge(kb.V.sem, kb.V.n)
                    if prev["rb_evac"]:
                        t.wait_ge(kb.V.sem, prev["rb_evac"])
                    kb.T.tag(t.matmul(p_rb[:], kb.ones_row[0:1, 0:128],
                                      den_r[0:1, :], start=True, stop=True))
                self.on("tensor", _)

                def _(v):
                    v.wait_ge(kb.T.sem, kb.T.n)
                    prev["rb_evac"] = kb.V.tag(v.tensor_copy(rb_sb[:], p_rb[:]))
                self.on("vector", _)

                def _(t, h=h, grp=grp, hc=hc, hr=hr):
                    t.wait_ge(kb.V.sem, kb.V.n)
                    if prev["o_evac"]:
                        t.wait_ge(kb.V.sem, prev["o_evac"])
                    ins = None
                    for kt in range(16):
                        ins = t.matmul(p_O[hr:hr + 64, :],
                                       v_tm[:, kt * D + h * 64:kt * D + (h + 1) * 64],
                                       PT[:, kt * 512:(kt + 1) * 512],
                                       start=(kt == 0), stop=(kt == 15))
                    prev["pv_done"] = kb.T.tag(ins)
                self.on("tensor", _)

                def _(v, h=h, grp=grp, hc=hc, hr=hr):
                    v.wait_ge(kb.T.sem, kb.T.n)
                    prev["o_evac"] = kb.V.tag(v.tensor_mul(
                        oT[hr:hr + 64, hc * S + grp * 512:hc * S + grp * 512 + 512],
                        p_O[hr:hr + 64, :], rb_sb[hr:hr + 64, :]))
                self.on("vector", _)

        self.dbg_dump("oT", oT[:], [128, 8 * S], BF16)
        actx.close()
        qctx.close()

        m2ctx = ExitStack()
        wo, wo_dq = self.load_w_lhsT(m2ctx, "wo", kb.w["L0_attn_Wo"], D, D)
        bo_row, bo_dq = self.load_row(m2ctx, "bo_row", kb.w["L0_attn_bo"], D)
        p_wo = m2ctx.enter_context(nc.psum_tensor("p2wo", [128, 512], F32))
        p_wo2 = m2ctx.enter_context(nc.psum_tensor("p2wo2", [128, 512], F32))
        s_tile = m2ctx.enter_context(sbt("p2st", [128, D], BF16))
        st_evac = [0]
        for ti in range(NT):
            for dc in range(2):
                def _(t, ti=ti, dc=dc):
                    t.wait_ge(wo_dq.sem, wo_dq.tok)
                    t.wait_ge(bo_dq.sem, bo_dq.tok)
                    t.wait_ge(kb.V.sem, kb.V.n)
                    pm = p_wo if dc == 0 else p_wo2
                    for hp in range(8):
                        t.matmul(pm[:],
                                 oT[:, hp * S + ti * 128:hp * S + (ti + 1) * 128],
                                 wo[:, hp * D + dc * 512:hp * D + (dc + 1) * 512],
                                 start=(hp == 0), stop=False)
                    ins = t.matmul(pm[:], kb.ones_row[0:1, 0:128],
                                   bo_row[0:1, dc * 512:(dc + 1) * 512],
                                   start=False, stop=True)
                    kb.T.tag(ins)
                self.on("tensor", _)

                def _(v, ti=ti, dc=dc):
                    v.wait_ge(kb.T.sem, kb.T.n)
                    if ti and dc == 0:
                        v.wait_ge(kb.dY.sem, kb.dY.tok)
                    pm = p_wo if dc == 0 else p_wo2
                    st_evac[0] = kb.V.tag(v.tensor_copy(
                        s_tile[:, dc * 512:(dc + 1) * 512], pm[:]))
                self.on("vector", _)

            def _(sync, ti=ti):
                sync.wait_ge(kb.V.sem, st_evac[0])
                kb.dY.start(sync, kb.d_s[ti * 128:(ti + 1) * 128, :], s_tile[:])
            self.on("sync", _)

        self.dbg_tap("s_attn", kb.d_s, [S, D], BF16)
        m2ctx.close()
        pctx.close()

    def p3(self):
        kb, nc = self, self.nc
        bar = self.bar()
        pctx = ExitStack()
        sbt = nc.sbuf_tensor
        TC = 1024
        NFC = 4096 // 128
        h_sb = pctx.enter_context(sbt("p3h", [128, NFC * TC], BF16))
        b1row = pctx.enter_context(sbt("p3b1r", [1, 4096], F32))
        b1col = pctx.enter_context(sbt("p3b1c", [128, 32], F32))
        b2row = pctx.enter_context(sbt("p3b2r", [1, D], F32))
        w1b = [pctx.enter_context(sbt(f"p3w1{p}", [128, 8 * 512], BF16))
               for p in range(2)]
        w2b = [pctx.enter_context(sbt(f"p3w2{p}", [128, 16 * D], BF16))
               for p in range(2)]
        ysc = [pctx.enter_context(sbt(f"p3y{p}", [128, 512], F32)) for p in range(2)]
        p_h = pctx.enter_context(nc.psum_tensor("p3ph", [128, 1024], F32))
        p_y = pctx.enter_context(nc.psum_tensor("p3py", [128, 1024], F32))
        p_bc = pctx.enter_context(nc.psum_tensor("p3pbc", [128, 32], F32))
        dq_w1 = [self.new_dq("w1_" + str(p)) for p in range(2)]
        dq_w2 = [self.new_dq("w2_" + str(p)) for p in range(2)]
        dq_b = self.new_dq("p3b")
        dq_acc = self.new_dq("p3acc")
        st = dict(w1_free=[0, 0], w2_free=[0, 0], h_used=0, b1c=0,
                  y_free=[0, 0], acc_round=0)

        for tc2 in range(2):
            for e in range(8):
                def _(sync, e=e, tc2=tc2):
                    if tc2 == 0 and e == 0:
                        self.wait_bar(sync, bar, skip=("dW",))
                    else:
                        sync.wait_ge(kb.SC.sem, kb.SC.n)
                        sync.wait_ge(kb.T.sem, kb.T.n)
                    dq_b.start(sync, b1row[0:1, :], kb.w["L0_ffn_b1"][e:e + 1, :])
                    dq_b.start(sync, b2row[0:1, :], kb.w["L0_ffn_b2"][e:e + 1, :])
                self.on("sync", _)

                def _(t, e=e, tc2=tc2):
                    t.wait_ge(dq_b.sem, dq_b.tok)
                    if st["b1c"]:
                        t.wait_ge(kb.V.sem, st["b1c"])
                    ins = None
                    for c in range(32):
                        ins = t.transpose(p_bc[:, c:c + 1],
                                          b1row[0:1, c * 128:(c + 1) * 128],
                                          kb.ident_f_sb[0:1, 0:1])
                    kb.T.tag(ins)
                self.on("tensor", _)

                def _(v, e=e):
                    v.wait_ge(kb.T.sem, kb.T.n)
                    st["b1c"] = kb.V.tag(v.tensor_copy(b1col[:], p_bc[:]))
                self.on("vector", _)
                b1c_tok = st["b1c"]

                for f8 in range(8):
                    p = f8 % 2

                    def _(g, e=e, tc2=tc2, f8=f8, p=p):
                        if st["w1_free"][p]:
                            g.wait_ge(kb.SC.sem, st["w1_free"][p])
                        elif tc2 == 0 and e == 0:
                            self.wait_bar(g, bar, skip=("G", "dW"))
                        for ic in range(8):
                            dq_w1[p].start(
                                g, w1b[p][:, ic * 512:(ic + 1) * 512],
                                kb.w["L0_ffn_w1"][e, ic * 128:(ic + 1) * 128,
                                                  f8 * 512:(f8 + 1) * 512])
                    self.on("gpsimd", _)

                    for fc4 in range(4):
                        fc = f8 * 4 + fc4
                        def _(t, e=e, tc2=tc2, f8=f8, fc4=fc4, fc=fc, p=p):
                            t.wait_ge(dq_w1[p].sem, dq_w1[p].tok)
                            t.wait_ge(kb.SC.sem, kb.SC.n)
                            if tc2 == 0 and e == 0 and f8 == 0 and fc4 == 0:
                                t.wait_ge(kb.V.sem, kb.V.n)
                            ins = None
                            for th in range(2):
                                for ic in range(8):
                                    ins = t.matmul(
                                        p_h[:, th * 512:(th + 1) * 512],
                                        w1b[p][:, ic * 512 + fc4 * 128:
                                               ic * 512 + (fc4 + 1) * 128],
                                        kb.xnT[:, ic * S + tc2 * TC + th * 512:
                                               ic * S + tc2 * TC + (th + 1) * 512],
                                        start=(ic == 0), stop=(ic == 7))
                            kb.T.tag(ins)
                        self.on("tensor", _)

                        def _(sc, fc=fc, p=p, last=(fc4 == 3)):
                            sc.wait_ge(kb.T.sem, kb.T.n)
                            sc.wait_ge(kb.V.sem, b1c_tok)
                            ins = sc.activation(
                                h_sb[:, fc * TC:(fc + 1) * TC], p_h[:],
                                AF.Gelu_apprx_tanh, bias=b1col[:, fc:fc + 1])
                            tok = kb.SC.tag(ins)
                            if last:
                                st["w1_free"][p] = tok
                        self.on("scalar", _)

                for half in range(2):
                    p = half

                    def _(g, e=e, tc2=tc2, half=half, p=p):
                        if st["w2_free"][p]:
                            g.wait_ge(kb.T.sem, st["w2_free"][p])
                        elif tc2 == 0 and e == 0:
                            self.wait_bar(g, bar, skip=("G", "dW"))
                        for fc16 in range(16):
                            fc = half * 16 + fc16
                            dq_w2[p].start(
                                g, w2b[p][:, fc16 * D:(fc16 + 1) * D],
                                kb.w["L0_ffn_w2"][e, fc * 128:(fc + 1) * 128, :])
                    self.on("gpsimd", _)

                    for tt in range(8):
                        for dh in range(2):
                            pslot = (tt * 2 + dh) % 2
                            def _(t, e=e, tc2=tc2, half=half, tt=tt, dh=dh,
                                  p=p, pslot=pslot):
                                t.wait_ge(dq_w2[p].sem, dq_w2[p].tok)
                                t.wait_ge(kb.SC.sem, kb.SC.n)
                                t.wait_ge(kb.V.sem, kb.V.n)
                                ins = None
                                for fc16 in range(16):
                                    fc = half * 16 + fc16
                                    ins = t.matmul(
                                        p_y[:, pslot * 512:(pslot + 1) * 512],
                                        h_sb[:, fc * TC + tt * 128:
                                             fc * TC + (tt + 1) * 128],
                                        w2b[p][:, fc16 * D + dh * 512:
                                               fc16 * D + (dh + 1) * 512],
                                        start=(fc16 == 0),
                                        stop=(fc16 == 15 and half == 0))
                                if half == 1:
                                    ins = t.matmul(
                                        p_y[:, pslot * 512:(pslot + 1) * 512],
                                        kb.ones_row[0:1, 0:128],
                                        b2row[0:1, dh * 512:(dh + 1) * 512],
                                        start=False, stop=True)
                                w2f = kb.T.tag(ins)
                                if tt == 7 and dh == 1:
                                    st["w2_free"][p] = w2f
                            self.on("tensor", _)

                            def _(v, e=e, tc2=tc2, tt=tt, dh=dh, pslot=pslot):
                                v.wait_ge(kb.T.sem, kb.T.n)
                                yp = pslot
                                if st["y_free"][yp]:
                                    v.wait_ge(dq_acc.sem, st["y_free"][yp])
                                ti_abs = tc2 * 8 + tt
                                kb.V.tag(v.tensor_scalar(
                                    ysc[yp][:], p_y[:, pslot * 512:(pslot + 1) * 512],
                                    kb.c_tm[:, ti_abs * 8 + e:ti_abs * 8 + e + 1],
                                    None, op0=ALU.mult))
                            self.on("vector", _)

                            def _(g, e=e, tc2=tc2, half=half, tt=tt, dh=dh,
                                  pslot=pslot):
                                g.wait_ge(kb.V.sem, kb.V.n)
                                if tt == 0 and dh == 0 and st["acc_round"]:
                                    g.wait_ge(dq_acc.sem, st["acc_round"])
                                first = (e == 0 and half == 0)
                                op = dict() if first else dict(accum_op=ALU.add)
                                tok = dq_acc.start(
                                    g, kb.d_acc[tc2 * TC + tt * 128:
                                                tc2 * TC + (tt + 1) * 128,
                                                dh * 512:(dh + 1) * 512],
                                    ysc[pslot][:], **op)
                                st["y_free"][pslot] = tok
                            self.on("gpsimd", _)
                    st["acc_round"] = dq_acc.tok


        self.dbg_tap("d_acc", kb.d_acc, [S, D])
        pctx.close()
        self.b0ctx.close()

    def p45(self):
        kb, nc = self, self.nc
        bar = self.bar()
        self.b1ctx = ExitStack()
        self.xn1rT = self.b1ctx.enter_context(
            nc.sbuf_tensor("xn1rT", [128, 8 * S], BF16))
        self.negrm_row = self.b1ctx.enter_context(
            nc.sbuf_tensor("negrm_row", [1, S], F32))
        self.ones2k = self.b1ctx.enter_context(
            nc.sbuf_tensor("ones2k", [1, S], F32))
        pctx = ExitStack()
        sbt = nc.sbuf_tensor
        xb = [pctx.enter_context(sbt(f"p4x{p}", [128, D], F32)) for p in range(2)]
        sb_ = [pctx.enter_context(sbt(f"p4s{p}", [128, D], BF16)) for p in range(2)]
        ab = [pctx.enter_context(sbt(f"p4a{p}", [128, D], F32)) for p in range(2)]
        gb = pctx.enter_context(sbt("p4g", [128, D], BF16))
        x1b = [pctx.enter_context(sbt(f"p4o{p}", [128, D], F32)) for p in range(2)]
        xn1b = pctx.enter_context(sbt("p4n", [128, D], BF16))
        xn1f = pctx.enter_context(sbt("p4nf", [128, D], F32))
        negmu2 = pctx.enter_context(sbt("p4mu2", [128, 1], F32))
        rstd2 = pctx.enter_context(sbt("p4rs2", [128, 1], F32))
        negrm = pctx.enter_context(sbt("p4nrm", [128, 1], F32))
        p_nr = pctx.enter_context(nc.psum_tensor("p4pnr", [1, 128], F32))
        sq = pctx.enter_context(sbt("p4sq", [128, D], F32))
        std = pctx.enter_context(sbt("p4std", [128, 1], F32))
        negmu = pctx.enter_context(sbt("p4mu", [128, 1], F32))
        rstd = pctx.enter_context(sbt("p4rs", [128, 1], F32))
        ptr = pctx.enter_context(nc.psum_tensor("p4tr", [128, D], BF16))
        ldx = [self.new_dq(f"p4x{p}") for p in range(2)]
        lds = [self.new_dq(f"p4s{p}") for p in range(2)]
        lda = [self.new_dq(f"p4a{p}") for p in range(2)]
        dout = self.new_dq("p4out")
        st = dict(x_free=[0, 0], s_free=[0, 0], a_free=[0, 0], o_free=[0, 0],
                  evac=0, tr=0)

        for i in range(NT):
            p = i % 2

            def _(sync, i=i, p=p):
                if i <= 1:
                    self.wait_bar(sync, bar, skip=("dW",))
                if st["x_free"][p]:
                    sync.wait_ge(kb.V.sem, st["x_free"][p])
                ldx[p].start(sync, xb[p][:], kb.x[i * 128:(i + 1) * 128, :])
                lds[p].start(sync, sb_[p][:], kb.d_s[i * 128:(i + 1) * 128, :])
                lda[p].start(sync, ab[p][:], kb.d_acc[i * 128:(i + 1) * 128, :])
            self.on("sync", _)

            def _(sc, i=i, p=p):
                sc.wait_ge(lda[p].sem, lda[p].tok)
                kb.SC.tag(sc.activation(gb[:], ab[p][:], AF.Silu))
            self.on("scalar", _)

            def _(v, i=i, p=p):
                v.wait_ge(kb.SC.sem, kb.SC.n)
                v.wait_ge(ldx[p].sem, ldx[p].tok)
                v.wait_ge(lds[p].sem, lds[p].tok)
                t0 = kb.V.tag(v.tensor_mul(sq[:], sb_[p][:], gb[:]))
                v.wait_ge(kb.V.sem, t0)
                if st["o_free"][p]:
                    v.wait_ge(dout.sem, st["o_free"][p])
                t1 = kb.V.tag(v.tensor_add(x1b[p][:], xb[p][:], sq[:]))
                st["x_free"][p] = t1
                v.wait_ge(kb.V.sem, t1)
                t2 = kb.V.tag(v.tensor_reduce(negmu[:], x1b[p][:], axis=AX.X,
                                              op=ALU.add, negate=True))
                v.wait_ge(kb.V.sem, t2)
                kb.V.tag(v.tensor_scalar_mul(negmu[:], negmu[:], 1.0 / D))
            self.on("vector", _)

            def _(sc, i=i, p=p):
                sc.wait_ge(kb.V.sem, kb.V.n)
                kb.SC.tag(sc.activation(sq[:], x1b[p][:], AF.Square,
                                        bias=negmu[:, 0:1]))
            self.on("scalar", _)

            def _(v, i=i):
                v.wait_ge(kb.SC.sem, kb.SC.n)
                kb.V.tag(v.tensor_reduce(std[:], sq[:], axis=AX.X, op=ALU.add))
            self.on("vector", _)

            def _(sc, i=i):
                sc.wait_ge(kb.V.sem, kb.V.n)
                kb.SC.tag(sc.activation(std[:], std[:], AF.Sqrt,
                                        scale=1.0 / D, bias=kb.epsb[:, 0:1]))
            self.on("scalar", _)

            def _(v, i=i, p=p):
                v.wait_ge(kb.SC.sem, kb.SC.n)
                v.wait_ge(dout.sem, dout.tok)
                t0 = kb.V.tag(v.reciprocal(rstd[:], std[:]))
                v.wait_ge(kb.V.sem, t0)
                t1 = kb.V.tag(v.tensor_scalar(
                    xn1f[:], x1b[p][:], negmu[:, 0:1], rstd[:, 0:1],
                    op0=ALU.add, op1=ALU.mult))
                st["x_free"][p] = max(st["x_free"][p], t1)
                v.wait_ge(kb.V.sem, t1)
                t2 = kb.V.tag(v.tensor_reduce(negmu2[:], xn1f[:], axis=AX.X,
                                              op=ALU.add, negate=True))
                v.wait_ge(kb.V.sem, t2)
                kb.V.tag(v.tensor_scalar_mul(negmu2[:], negmu2[:], 1.0 / D))
            self.on("vector", _)

            def _(sync, i=i, p=p):
                sync.wait_ge(kb.V.sem, kb.V.n)
                st["o_free"][p] = dout.start(
                    sync, kb.d_x1[i * 128:(i + 1) * 128, :], x1b[p][:])
                dout.start(sync, kb.d_xn1[i * 128:(i + 1) * 128, :], xn1f[:])
            self.on("sync", _)

            def _(sc, i=i):
                sc.wait_ge(kb.V.sem, kb.V.n)
                kb.SC.tag(sc.activation(sq[:], xn1f[:], AF.Square,
                                        bias=negmu2[:, 0:1]))
            self.on("scalar", _)

            def _(v, i=i):
                v.wait_ge(kb.SC.sem, kb.SC.n)
                kb.V.tag(v.tensor_reduce(std[:], sq[:], axis=AX.X, op=ALU.add))
            self.on("vector", _)

            def _(sc, i=i):
                sc.wait_ge(kb.V.sem, kb.V.n)
                kb.SC.tag(sc.activation(std[:], std[:], AF.Sqrt,
                                        scale=1.0 / D, bias=kb.epsb[:, 0:1]))
            self.on("scalar", _)

            def _(v, i=i, p=p):
                v.wait_ge(kb.SC.sem, kb.SC.n)
                t0 = kb.V.tag(v.reciprocal(rstd2[:], std[:]))
                v.wait_ge(kb.V.sem, t0)
                t1 = kb.V.tag(v.tensor_mul(negrm[:], negmu2[:], rstd2[:]))
                if st["tr"]:
                    v.wait_ge(kb.T.sem, st["tr"])
                t2 = kb.V.tag(v.tensor_scalar(
                    xn1b[:], xn1f[:], negmu2[:, 0:1], rstd2[:, 0:1],
                    op0=ALU.add, op1=ALU.mult))
                v.wait_ge(kb.V.sem, t2)
            self.on("vector", _)

            def _(t, i=i, p=p):
                t.wait_ge(kb.V.sem, kb.V.n)
                if st["evac"]:
                    t.wait_ge(kb.V.sem, st["evac"])
                ins = None
                for c in range(8):
                    ins = t.transpose(ptr[:, c * 128:(c + 1) * 128],
                                      xn1b[:, c * 128:(c + 1) * 128],
                                      kb.ident_b_sb[:])
                st["tr"] = kb.T.tag(ins)
                kb.T.tag(t.transpose(p_nr[0:1, :], negrm[:], kb.ident_f_sb[:]))
            self.on("tensor", _)

            def _(v, i=i):
                v.wait_ge(kb.T.sem, kb.T.n)
                t0 = kb.V.tag(v.tensor_copy(
                    kb.xn1rT[:].rearrange("q (c t) -> q c t", c=8)
                    [:, :, i * 128:(i + 1) * 128],
                    ptr[:].rearrange("q (c t) -> q c t", c=8)))
                kb.V.tag(v.tensor_copy(
                    kb.negrm_row[0:1, i * 128:(i + 1) * 128], p_nr[0:1, :]))
                st["evac"] = kb.V.n
            self.on("vector", _)

        def _(v):
            kb.V.tag(v.memset(kb.ones2k[:], 1.0))
        self.on("vector", _)
        self.dbg_tap("x1", kb.d_x1, [S, D])
        self.dbg_tap("xn1", kb.d_xn1, [S, D])
        pctx.close()

    def p6(self):
        kb, nc = self, self.nc
        bar = self.bar()
        pctx = ExitStack()
        sbt = nc.sbuf_tensor
        gate_tm = pctx.enter_context(sbt("p6gate", [128, NT * D], BF16))
        delta_fm = pctx.enter_context(sbt("p6delta", [16, S], F32))
        B_fm = pctx.enter_context(sbt("p6B", [16, S], F32))
        hs_fm = pctx.enter_context(sbt("p6hs", [16, S], F32))
        hd = pctx.enter_context(sbt("p6hd", [16, 1], F32))
        stack2 = pctx.enter_context(sbt("p6stk", [2, S], F32))
        rstack = pctx.enter_context(sbt("p6rstk", [2, D], F32))
        dstk_d = pctx.enter_context(sbt("p6dstkd", [2, 16], F32))
        dstk_i = pctx.enter_context(sbt("p6dstki", [2, 16], F32))
        tmp_row = pctx.enter_context(sbt("p6tmpr", [1, D], F32))
        A_sb = pctx.enter_context(sbt("p6A", [16, 16], F32))
        C_sb = pctx.enter_context(sbt("p6C", [16, D], F32))
        yg_sb = [pctx.enter_context(sbt(f"p6yg{p}", [128, 512], BF16))
                 for p in range(2)]
        p_r = pctx.enter_context(nc.psum_tensor("p6pr", [1, 512], F32))
        p_r2 = pctx.enter_context(nc.psum_tensor("p6pr2", [1, 512], F32))
        p_bc = pctx.enter_context(nc.psum_tensor("p6pbc", [128, 32], F32))
        p_mm = pctx.enter_context(nc.psum_tensor("p6pmm", [128, 512], F32))
        p_mm2 = pctx.enter_context(nc.psum_tensor("p6pmm2", [128, 512], F32))
        p_d = pctx.enter_context(nc.psum_tensor("p6pd", [16, 512], F32))
        p_hA = pctx.enter_context(nc.psum_tensor("p6phA", [16, 1], F32))
        p_hB = pctx.enter_context(nc.psum_tensor("p6phB", [16, 1], F32))

        wg, wg_dq = self.load_w_lhsT(pctx, "p6wg", kb.w["L1_attn_w_gate"], D, D)
        wd, wd_dq = self.load_w_lhsT(pctx, "p6wd", kb.w["L1_attn_w_delta"], D, 16)
        wi, wi_dq = self.load_w_lhsT(pctx, "p6wi", kb.w["L1_attn_w_in"], D, 16)
        gam_row, gam_dq = self.load_row(pctx, "p6gam", kb.w["L1_attn_ln_gamma"], D)
        bet_row, bet_dq = self.load_row(pctx, "p6bet", kb.w["L1_attn_ln_beta"], D)
        bg_row, bg_dq = self.load_row(pctx, "p6bg", kb.w["L1_attn_b_gate"], D)
        bd_row, bd_dq = self.load_row(pctx, "p6bd", kb.w["L1_attn_b_delta"], 16)
        bi_row, bi_dq = self.load_row(pctx, "p6bi", kb.w["L1_attn_b_in"], 16)
        gam_col = self.row_to_cols(pctx, "p6gamc", gam_row, D, gam_dq, p_bc)
        bet_col = self.row_to_cols(pctx, "p6betc", bet_row, D, bet_dq, p_bc)
        gam_colb = pctx.enter_context(sbt("p6gamcb", [128, 8], BF16))
        bet_colb = pctx.enter_context(sbt("p6betcb", [128, 8], BF16))

        def _(v):
            v.wait_ge(kb.V.sem, kb.V.n)
            kb.V.tag(v.tensor_copy(gam_colb[:], gam_col[:]))
            kb.V.tag(v.tensor_copy(bet_colb[:], bet_col[:]))
        self.on("vector", _)
        dq_ac = self.new_dq("p6AC")
        dq_s2 = self.new_dq("p6s2")
        dq_yg = self.new_dq("p6yg")

        def _(sync):
            self.wait_bar(sync, bar, skip=("dW",))
            dq_ac.start(sync, A_sb[:], kb.w["L1_attn_A"])
            dq_ac.start(sync, C_sb[:], kb.w["L1_attn_C"])
            dq_s2.start(sync, stack2[1:2, :], kb.ones2k[0:1, :])
        self.on("sync", _)

        def _(v):
            v.wait_ge(dq_s2.sem, dq_s2.tok)
            kb.V.tag(v.tensor_copy(stack2[0:1, :], kb.negrm_row[0:1, :]))
        self.on("vector", _)

        for oh in range(2):
            def _(t, oh=oh):
                t.wait_ge(wg_dq.sem, wg_dq.tok)
                t.wait_ge(kb.V.sem, kb.V.n)
                ins = None
                for ic in range(8):
                    t.matmul(p_r[0:1, :],
                             gam_colb[:, ic:ic + 1],
                             wg[:, ic * D + oh * 512:ic * D + (oh + 1) * 512],
                             start=(ic == 0), stop=(ic == 7))
                    ins = t.matmul(
                        p_r2[0:1, :],
                        bet_colb[:, ic:ic + 1],
                        wg[:, ic * D + oh * 512:ic * D + (oh + 1) * 512],
                        start=(ic == 0), stop=(ic == 7))
                kb.T.tag(ins)
            self.on("tensor", _)

            def _(v, oh=oh):
                v.wait_ge(kb.T.sem, kb.T.n)
                v.wait_ge(bg_dq.sem, bg_dq.tok)
                t0 = kb.V.tag(v.tensor_copy(rstack[0:1, oh * 512:(oh + 1) * 512],
                                            p_r[0:1, :]))
                t1 = kb.V.tag(v.tensor_add(tmp_row[0:1, oh * 512:(oh + 1) * 512],
                                           p_r2[0:1, :],
                                           bg_row[0:1, oh * 512:(oh + 1) * 512]))
                v.wait_ge(kb.V.sem, t1)
            self.on("vector", _)

        dq_row1g = self.new_dq("p6row1g")

        def _(sync):
            sync.wait_ge(kb.V.sem, kb.V.n)
            dq_row1g.start(sync, rstack[1:2, :], tmp_row[0:1, :])
        self.on("sync", _)

        def _(v):
            v.wait_ge(kb.T.sem, kb.T.n)
            for ic in range(8):
                t0 = kb.V.tag(v.tensor_scalar_mul(
                    wg[:, ic * D:(ic + 1) * D], wg[:, ic * D:(ic + 1) * D],
                    gam_col[:, ic:ic + 1]))
                v.wait_ge(kb.V.sem, t0)
        self.on("vector", _)
        wgs_tok = kb.V.n

        st = dict(gev=[0, 0])
        for ti in range(NT):
            for oh in range(2):
                pm = p_mm if oh == 0 else p_mm2

                def _(t, ti=ti, oh=oh, pm=pm):
                    t.wait_ge(kb.V.sem, wgs_tok)
                    t.wait_ge(dq_row1g.sem, dq_row1g.tok)
                    if st["gev"][oh]:
                        t.wait_ge(kb.SC.sem, st["gev"][oh])
                    for ic in range(8):
                        t.matmul(pm[:],
                                 kb.xn1rT[:, ic * S + ti * 128:ic * S + (ti + 1) * 128],
                                 wg[:, ic * D + oh * 512:ic * D + (oh + 1) * 512],
                                 start=(ic == 0), stop=False)
                    ins = t.matmul(pm[:], stack2[0:2, ti * 128:(ti + 1) * 128],
                                   rstack[0:2, oh * 512:(oh + 1) * 512],
                                   start=False, stop=True)
                    kb.T.tag(ins)
                self.on("tensor", _)

                def _(sc, ti=ti, oh=oh, pm=pm):
                    sc.wait_ge(kb.T.sem, kb.T.n)
                    st["gev"][oh] = kb.SC.tag(sc.activation(
                        gate_tm[:, ti * D + oh * 512:ti * D + (oh + 1) * 512],
                        pm[:], AF.Sigmoid))
                self.on("scalar", _)

        def dproj(wslab, w_dq, brow, b_dq, dstk, dst, func):
            def _(t):
                t.wait_ge(w_dq.sem, w_dq.tok)
                t.wait_ge(kb.V.sem, kb.V.n)
                ins = None
                for ic in range(8):
                    t.matmul(p_r[0:1, 0:16], gam_colb[:, ic:ic + 1],
                             wslab[:, ic * 16:(ic + 1) * 16],
                             start=(ic == 0), stop=(ic == 7))
                    ins = t.matmul(p_r2[0:1, 0:16], bet_colb[:, ic:ic + 1],
                                   wslab[:, ic * 16:(ic + 1) * 16],
                                   start=(ic == 0), stop=(ic == 7))
                kb.T.tag(ins)
            self.on("tensor", _)

            def _(v):
                v.wait_ge(kb.T.sem, kb.T.n)
                v.wait_ge(b_dq.sem, b_dq.tok)
                t0 = kb.V.tag(v.tensor_copy(dstk[0:1, :], p_r[0:1, 0:16]))
                t1 = kb.V.tag(v.tensor_add(tmp_row[0:1, 0:16], p_r2[0:1, 0:16],
                                           brow[0:1, :]))
                v.wait_ge(kb.V.sem, t1)
            self.on("vector", _)

            dq_row1d = self.new_dq("p6row1d")

            def _(sync):
                sync.wait_ge(kb.V.sem, kb.V.n)
                dq_row1d.start(sync, dstk[1:2, :], tmp_row[0:1, 0:16])
            self.on("sync", _)

            def _(v):
                for ic in range(8):
                    t0 = kb.V.tag(v.tensor_scalar_mul(
                        wslab[:, ic * 16:(ic + 1) * 16],
                        wslab[:, ic * 16:(ic + 1) * 16], gam_col[:, ic:ic + 1]))
                    v.wait_ge(kb.V.sem, t0)
            self.on("vector", _)
            ws_tok = kb.V.n

            ev = [0]
            for tch in range(4):
                def _(t, tch=tch):
                    t.wait_ge(kb.V.sem, ws_tok)
                    t.wait_ge(dq_row1d.sem, dq_row1d.tok)
                    if ev[0]:
                        t.wait_ge(kb.SC.sem, ev[0]) if func else \
                            t.wait_ge(kb.V.sem, ev[0])
                    for ic in range(8):
                        t.matmul(p_d[:], wslab[:, ic * 16:(ic + 1) * 16],
                                 kb.xn1rT[:, ic * S + tch * 512:ic * S + (tch + 1) * 512],
                                 start=(ic == 0), stop=False)
                    ins = t.matmul(p_d[:], dstk[0:2, 0:16],
                                   stack2[0:2, tch * 512:(tch + 1) * 512],
                                   start=False, stop=True)
                    kb.T.tag(ins)
                self.on("tensor", _)

                if func:
                    def _(sc, tch=tch):
                        sc.wait_ge(kb.T.sem, kb.T.n)
                        ev[0] = kb.SC.tag(sc.activation(
                            dst[:, tch * 512:(tch + 1) * 512], p_d[:], func))
                    self.on("scalar", _)
                else:
                    def _(v, tch=tch):
                        v.wait_ge(kb.T.sem, kb.T.n)
                        ev[0] = kb.V.tag(v.tensor_copy(
                            dst[:, tch * 512:(tch + 1) * 512], p_d[:]))
                    self.on("vector", _)

        dproj(wd, wd_dq, bd_row, bd_dq, dstk_d, delta_fm, AF.Sigmoid)
        dproj(wi, wi_dq, bi_row, bi_dq, dstk_i, B_fm, None)

        def _(sc):
            sc.wait_ge(kb.V.sem, kb.V.n)
            sc.wait_ge(kb.T.sem, kb.T.n)
            sc.wait_ge(dq_ac.sem, dq_ac.tok)
            kb.SC.tag(sc.activation(hs_fm[:, 0:1], B_fm[:, 0:1], AF.Tanh))
        self.on("scalar", _)
        s0 = kb.SC.n
        v0 = kb.V.n
        t0_ = kb.T.n

        NCH = 256
        for c0 in range(1, S, NCH):
            c1 = min(c0 + NCH, S)

            def _(v, c0=c0, c1=c1):
                for t_ in range(c0, c1):
                    v.wait_ge(kb.SC.sem, s0 + t_ - 1)
                    kb.V.tag(v.tensor_mul(
                        hd[:], hs_fm[:, t_ - 1:t_], delta_fm[:, t_:t_ + 1]))
            self.on("vector", _)

            def _(t, c0=c0, c1=c1):
                for t_ in range(c0, c1):
                    t.wait_ge(kb.V.sem, v0 + t_)
                    ph = p_hA if t_ % 2 == 0 else p_hB
                    kb.T.tag(t.matmul(ph[:], A_sb[:], hd[:],
                                      start=True, stop=True))
            self.on("tensor", _)

            def _(sc, c0=c0, c1=c1):
                for t_ in range(c0, c1):
                    sc.wait_ge(kb.T.sem, t0_ + t_)
                    ph = p_hA if t_ % 2 == 0 else p_hB
                    kb.SC.tag(sc.activation(
                        hs_fm[:, t_:t_ + 1], ph[:], AF.Tanh,
                        bias=B_fm[:, t_:t_ + 1]))
            self.on("scalar", _)
        assert kb.SC.n == s0 + S - 1 and kb.V.n == v0 + S - 1 \
            and kb.T.n == t0_ + S - 1

        ygf = [0, 0]
        for ti in range(NT):
            for dh in range(2):
                pm = p_mm if dh == 0 else p_mm2

                def _(t, ti=ti, dh=dh, pm=pm):
                    t.wait_ge(kb.SC.sem, kb.SC.n)
                    t.wait_ge(kb.V.sem, kb.V.n)
                    kb.T.tag(t.matmul(pm[:], hs_fm[:, ti * 128:(ti + 1) * 128],
                                      C_sb[:, dh * 512:(dh + 1) * 512],
                                      start=True, stop=True))
                self.on("tensor", _)

                def _(v, ti=ti, dh=dh, pm=pm):
                    v.wait_ge(kb.T.sem, kb.T.n)
                    if ygf[dh]:
                        v.wait_ge(dq_yg.sem, ygf[dh])
                    kb.V.tag(v.tensor_mul(
                        yg_sb[dh][:], pm[:],
                        gate_tm[:, ti * D + dh * 512:ti * D + (dh + 1) * 512]))
                self.on("vector", _)

                def _(sync, ti=ti, dh=dh):
                    sync.wait_ge(kb.V.sem, kb.V.n)
                    ygf[dh] = dq_yg.start(
                        sync, kb.d_yg[ti * 128:(ti + 1) * 128,
                                      dh * 512:(dh + 1) * 512], yg_sb[dh][:])
                self.on("sync", _)

        self.dbg_tap("ygate", kb.d_yg, [S, D], BF16)
        pctx.close()

    P7STAGE = 99

    def p7(self, nlayers=6):
        kb, nc = self, self.nc
        self.b1ctx.close()
        bar = self.bar()
        pctx = ExitStack()
        sbt = nc.sbuf_tensor
        FT = S + 1
        fT = pctx.enter_context(sbt("p7fT", [128, 8 * FT], BF16))
        fbuf = [pctx.enter_context(sbt(f"p7fb{p}", [128, D], F32))
                for p in range(2)]
        fnew = [pctx.enter_context(sbt(f"p7fn{p}", [128, D], F32))
                for p in range(2)]
        dq_fb = [self.new_dq(f"p7fb{p}") for p in range(2)]
        dq_fo = self.new_dq("p7fo")
        onescol_b = pctx.enter_context(sbt("p7ones", [128, 1], BF16))
        csq = pctx.enter_context(sbt("p7csq", [128, NT * 64], BF16))
        ssq = pctx.enter_context(sbt("p7ssq", [128, NT * 64], BF16))
        cik = pctx.enter_context(sbt("p7cik", [128, NT * 64], BF16))
        sik = pctx.enter_context(sbt("p7sik", [128, NT * 64], BF16))
        rotraw = pctx.enter_context(sbt("p7rot", [128, NT * 64], F32))
        qs_sb = pctx.enter_context(sbt("p7qs", [128, 64], F32))
        qsr_sb = pctx.enter_context(sbt("p7qsr", [128, 64], F32))
        ks_sb = pctx.enter_context(sbt("p7ks", [128, 64], F32))
        ksr_sb = pctx.enter_context(sbt("p7ksr", [128, 64], F32))
        srow64 = pctx.enter_context(sbt("p7sr64", [1, 64], F32))
        mmain = pctx.enter_context(sbt("p7mm", [128, 384], F32))
        m0 = pctx.enter_context(sbt("p7m0", [128, 128], F32))
        m1_ = pctx.enter_context(sbt("p7m1", [128, 256], F32))
        cast_b = pctx.enter_context(sbt("p7cast", [128, D], BF16))

        p_big = pctx.enter_context(nc.psum_tensor("p7big", [128, 3072], F32))
        p_tr = pctx.enter_context(nc.psum_tensor("p7ptr", [128, 1024], BF16))
        p_misc = pctx.enter_context(nc.psum_tensor("p7pm", [128, 512], F32))
        p_att = p_big[:, 0:2048]
        p_O7 = p_big[:, 2048:2560]
        p_wo7 = p_big[:, 2560:3072]
        p_wo7b = p_big[:, 0:512]
        p_a7 = p_big[:, 0:512]
        p_g7 = p_big[:, 512:1024]
        p_w2o = p_big[:, 1024:1536]
        p_w2ob = p_big[:, 1536:2048]
        p_bc7 = p_misc[:, 0:64]
        p_r7 = p_misc[0:1, 0:512]

        dq_tab = self.new_dq("p7tab")
        dq_qs = self.new_dq("p7qs")
        dq_msk = self.new_dq("p7msk")
        dq_r1 = [self.new_dq("p7r1a"), self.new_dq("p7r1b")]

        def _(sync):
            self.wait_bar(sync, bar, skip=("dW",))
            dq_msk.start(sync, mmain[:], kb.mask_main)
            dq_msk.start(sync, m0[:], kb.mask_a0)
            dq_msk.start(sync, m1_[:], kb.mask_a1)
        self.on("sync", _)

        def _(v):
            self.wait_bar(v, bar, skip=("V", "dW"))
            kb.V.tag(v.memset(onescol_b[:], 1.0))
        self.on("vector", _)

        W2C = [128] * 21 + [42]
        J_SZ = [512] * 5 + [170]

        def transpose_f_pass(src_dram):
            st = dict(tr=0, ev=0, vfree=[0, 0])

            def _(v):
                v.wait_ge(kb.T.sem, kb.T.n)
                kb.V.tag(v.memset(
                    fT[:].rearrange("p (c w) -> p c w", c=8)[:, :, 0:1], 0.0))
            self.on("vector", _)

            for i in range(NT):
                p = i % 2

                def _(sync, i=i, p=p, src_dram=src_dram):
                    if st["vfree"][p]:
                        sync.wait_ge(kb.V.sem, st["vfree"][p])
                    else:
                        sync.wait_ge(kb.V.sem, kb.V.n)
                        sync.wait_ge(kb.T.sem, kb.T.n)
                        if dq_fo.n:
                            sync.wait_ge(dq_fo.sem, dq_fo.tok)
                    dq_fb[p].start(sync, fbuf[p][:],
                                   src_dram[i * 128:(i + 1) * 128, :])
                self.on("sync", _)

                def _(v, i=i, p=p):
                    v.wait_ge(dq_fb[p].sem, dq_fb[p].tok)
                    if st["tr"]:
                        v.wait_ge(kb.T.sem, st["tr"])
                    st["vfree"][p] = kb.V.tag(v.tensor_copy(cast_b[:],
                                                            fbuf[p][:]))
                self.on("vector", _)

                def _(t, i=i):
                    t.wait_ge(kb.V.sem, kb.V.n)
                    if st["ev"]:
                        t.wait_ge(kb.V.sem, st["ev"])
                    ins = None
                    for c in range(8):
                        ins = t.transpose(p_tr[:, c * 128:(c + 1) * 128],
                                          cast_b[:, c * 128:(c + 1) * 128],
                                          kb.ident_b_sb[:])
                    st["tr"] = kb.T.tag(ins)
                self.on("tensor", _)

                def _(v, i=i):
                    v.wait_ge(kb.T.sem, kb.T.n)
                    st["ev"] = kb.V.tag(v.tensor_copy(
                        fT[:].rearrange("p (c w) -> p c w", c=8)
                        [:, :, 1 + i * 128:1 + (i + 1) * 128],
                        p_tr[:].rearrange("p (c w) -> p c w", c=8)))
                self.on("vector", _)

        def xsl(c, t0_, w, shift=True):
            off = 0 if (shift and c >= 4) else 1
            return fT[:, c * FT + off + t0_:c * FT + off + t0_ + w]

        for li in range(nlayers):
            lw = kb.w
            wctx = ExitStack()
            wo, wo_dq = self.load_w_lhsT(
                wctx, f"l{li}wo", lw["L1_ffn_Wo"][li], D, D)

            def qs_build(name_row, dst, dst_r):
                def _(sync):
                    sync.wait_ge(kb.T.sem, kb.T.n)
                    sync.wait_ge(kb.V.sem, kb.V.n)
                    dq_qs.start(sync, srow64[0:1, :],
                                lw[name_row][li, :].unsqueeze(0))
                self.on("sync", _)

                def _(t):
                    t.wait_ge(dq_qs.sem, dq_qs.tok)
                    t.wait_ge(kb.V.sem, kb.V.n)
                    kb.T.tag(t.matmul(p_bc7[:, 0:64], kb.ones_row[0:1, 0:128],
                                      srow64[0:1, :], start=True, stop=True))
                self.on("tensor", _)

                def _(v):
                    v.wait_ge(kb.T.sem, kb.T.n)
                    t0 = kb.V.tag(v.tensor_copy(dst[:], p_bc7[:, 0:64]))
                    v.wait_ge(kb.V.sem, t0)
                    kb.V.tag(v.tensor_copy(dst_r[:, 0:32], dst[:, 32:64]))
                    t1 = kb.V.tag(v.tensor_copy(dst_r[:, 32:64], dst[:, 0:32]))
                    v.wait_ge(kb.V.sem, t1)
                self.on("vector", _)

            qs_build("L1_ffn_q_scale", qs_sb, qsr_sb)
            qs_build("L1_ffn_k_scale", ks_sb, ksr_sb)

            rctx = ExitStack()
            rotraw = rctx.enter_context(sbt(f"l{li}rot", [128, NT * 64], F32))
            for src_t, dst_t, scv in [
                    (kb.rot_cs, csq, qs_sb), (kb.rot_ss, ssq, qsr_sb),
                    (kb.rot_ci, cik, ks_sb), (kb.rot_si, sik, ksr_sb)]:
                def _(sync, src_t=src_t):
                    sync.wait_ge(kb.V.sem, kb.V.n)
                    dq_tab.start(sync,
                                 rotraw[:].rearrange("p (i d) -> p i d", i=NT),
                                 src_t.rearrange("(i p) d -> p i d", p=128))
                self.on("sync", _)

                def _(v, dst_t=dst_t, scv=scv):
                    v.wait_ge(dq_tab.sem, dq_tab.tok)
                    t0 = kb.V.tag(v.tensor_mul(
                        dst_t[:].rearrange("p (i d) -> p i d", i=NT),
                        rotraw[:].rearrange("p (i d) -> p i d", i=NT),
                        scv[:].unsqueeze(1).broadcast_to([128, NT, 64])))
                    v.wait_ge(kb.V.sem, t0)
                self.on("vector", _)
            rctx.close()

            transpose_f_pass(kb.d_xn1 if li == 0 else kb.d_f)

            if self.P7STAGE < 1:
                self.dbg_dump("fT_dbg", fT[:], [128, 8 * FT], BF16)
                return
            actx = ExitStack()
            wqkv_b = [actx.enter_context(sbt(f"l{li}wq{p}", [128, 8 * 512], BF16))
                      for p in range(2)]
            qkv_g = actx.enter_context(sbt(f"l{li}qkvg", [128, 4 * 3072], BF16))
            scr = actx.enter_context(sbt(f"l{li}scr", [128, D], F32))
            scr2 = actx.enter_context(sbt(f"l{li}scr2", [128, D], F32))
            ss16 = actx.enter_context(sbt(f"l{li}ss", [128, 16], F32))
            qfb = actx.enter_context(sbt(f"l{li}qfb", [128, D], BF16))
            kfb = actx.enter_context(sbt(f"l{li}kfb", [128, D], BF16))
            qT = actx.enter_context(sbt(f"l{li}qT", [128, 8 * 128], BF16))
            kTr = actx.enter_context(sbt(f"l{li}kTr", [128, 8 * 3 * 128], BF16))
            vr = actx.enter_context(sbt(f"l{li}vr", [128, 3 * D], BF16))
            masked = actx.enter_context(sbt(f"l{li}mskd", [128, 4 * 384], F32))
            negmx = actx.enter_context(sbt(f"l{li}ngm", [128, 16], F32))
            den = actx.enter_context(sbt(f"l{li}den", [128, 16], F32))
            rden = actx.enter_context(sbt(f"l{li}rden", [128, 16], F32))
            PT_hg = actx.enter_context(sbt(f"l{li}PT", [128, 4 * 384], BF16))
            ptb = actx.enter_context(sbt(f"l{li}ptb", [128, 384], BF16))
            oT_t = actx.enter_context(sbt(f"l{li}oT", [128, 8 * 128], BF16))
            dq_wqkv = [self.new_dq("p7wqkv0"), self.new_dq("p7wqkv1")]

            st = dict(attn_done=0, exp=0, ptev=0, oev=0, qkv_ev=0,
                      wq_free=[0, 0])
            for tg in range(4):
                for n in range(6):
                    p = n % 2

                    def _(g, tg=tg, n=n, p=p, li=li):
                        if st["wq_free"][p]:
                            g.wait_ge(kb.T.sem, st["wq_free"][p])
                        else:
                            g.wait_ge(kb.T.sem, kb.T.n)
                            g.wait_ge(kb.V.sem, kb.V.n)
                        for ic in range(8):
                            dq_wqkv[p].start(
                                g, wqkv_b[p][:, ic * 512:(ic + 1) * 512],
                                lw["L1_ffn_Wqkv"][li, ic * 128:(ic + 1) * 128,
                                                  n * 512:(n + 1) * 512])
                    self.on("gpsimd", _)

                    for a4 in range(4):
                        a = tg * 4 + a4

                        def _(t, a=a, n=n, p=p, a4=a4):
                            t.wait_ge(dq_wqkv[p].sem, dq_wqkv[p].tok)
                            t.wait_ge(kb.V.sem, kb.V.n)
                            ins = None
                            pm = p_misc if (a4 % 2 == 0) else p_misc
                            pm = p_big[:, 0:512] if (a4 % 2 == 0) else \
                                p_big[:, 512:1024]
                            for ic in range(8):
                                ins = t.matmul(
                                    pm, xsl(ic, a * 128, 128),
                                    wqkv_b[p][:, ic * 512:(ic + 1) * 512],
                                    start=(ic == 0), stop=(ic == 7))
                            if a4 == 3 and True:
                                st["wq_free"][p] = kb.T.tag(ins)
                            else:
                                kb.T.tag(ins)
                        self.on("tensor", _)

                        def _(v, a=a, n=n, a4=a4):
                            v.wait_ge(kb.T.sem, kb.T.n)
                            pm = p_big[:, 0:512] if (a4 % 2 == 0) else \
                                p_big[:, 512:1024]
                            st["qkv_ev"] = kb.V.tag(v.tensor_copy(
                                qkv_g[:, a4 * 3072 + n * 512:
                                      a4 * 3072 + (n + 1) * 512], pm))
                        self.on("vector", _)

                if self.P7STAGE < 2 and tg == 0:
                    self.dbg_dump("qkvg_dbg", qkv_g[:], [128, 4 * 3072], BF16)
                    return
                for a4 in range(4):
                    a = tg * 4 + a4

                    def _(v, a=a, a4=a4):
                        v.wait_ge(kb.V.sem, kb.V.n)
                        t0 = kb.V.tag(v.tensor_copy(
                            vr[:, (a % 3) * D:(a % 3 + 1) * D],
                            qkv_g[:, a4 * 3072 + 2048:a4 * 3072 + 3072]))
                        v.wait_ge(kb.V.sem, t0)
                    self.on("vector", _)

                    def rotq(src_ap, cs_t, ss_t, dstb, a=a):
                        def _(sc, src_ap=src_ap):
                            sc.wait_ge(kb.V.sem, kb.V.n)
                            kb.SC.tag(sc.activation(scr[:], src_ap, AF.Square))
                        self.on("scalar", _)

                        def _(v):
                            v.wait_ge(kb.SC.sem, kb.SC.n)
                            t0 = kb.V.tag(v.tensor_reduce(
                                ss16[:], scr[:].rearrange("p (h d) -> p h d", h=16),
                                axis=AX.X, op=ALU.add))
                            v.wait_ge(kb.V.sem, t0)
                        self.on("vector", _)

                        def _(sc):
                            sc.wait_ge(kb.V.sem, kb.V.n)
                            kb.SC.tag(sc.activation(ss16[:], ss16[:], AF.Sqrt))
                        self.on("scalar", _)

                        def _(v, src_ap=src_ap, cs_t=cs_t, ss_t=ss_t,
                              dstb=dstb, a=a):
                            v.wait_ge(kb.SC.sem, kb.SC.n)
                            t0 = kb.V.tag(v.tensor_scalar_max(ss16[:], ss16[:],
                                                              1e-12))
                            v.wait_ge(kb.V.sem, t0)
                            t1 = kb.V.tag(v.reciprocal(ss16[:], ss16[:]))
                            v.wait_ge(kb.V.sem, t1)
                            t2 = kb.V.tag(v.tensor_mul(
                                scr[:].rearrange("p (h d) -> p h d", h=16),
                                src_ap.rearrange("p (h d) -> p h d", h=16),
                                ss16[:].unsqueeze(2).broadcast_to([128, 16, 64])))
                            v.wait_ge(kb.V.sem, t2)
                            t3 = kb.V.tag(v.tensor_scalar_mul(
                                scr2[:].rearrange("p (h x) -> p h x", h=16)
                                [:, :, 0:32],
                                scr[:].rearrange("p (h x) -> p h x", h=16)
                                [:, :, 32:64], -1.0))
                            t3 = kb.V.tag(v.tensor_copy(
                                scr2[:].rearrange("p (h x) -> p h x", h=16)
                                [:, :, 32:64],
                                scr[:].rearrange("p (h x) -> p h x", h=16)
                                [:, :, 0:32]))
                            v.wait_ge(kb.V.sem, t3)
                            t5 = kb.V.tag(v.tensor_mul(
                                scr[:].rearrange("p (h d) -> p h d", h=16),
                                scr[:].rearrange("p (h d) -> p h d", h=16),
                                cs_t[:, a * 64:(a + 1) * 64].unsqueeze(1)
                                .broadcast_to([128, 16, 64])))
                            t6 = kb.V.tag(v.tensor_mul(
                                scr2[:].rearrange("p (h d) -> p h d", h=16),
                                scr2[:].rearrange("p (h d) -> p h d", h=16),
                                ss_t[:, a * 64:(a + 1) * 64].unsqueeze(1)
                                .broadcast_to([128, 16, 64])))
                            v.wait_ge(kb.V.sem, t6)
                            t7 = kb.V.tag(v.tensor_add(dstb[:], scr[:], scr2[:]))
                            v.wait_ge(kb.V.sem, t7)
                        self.on("vector", _)

                    rotq(qkv_g[:, a4 * 3072:a4 * 3072 + 1024], csq, ssq, qfb)
                    rotq(qkv_g[:, a4 * 3072 + 1024:a4 * 3072 + 2048], cik, sik,
                         kfb)
                    if self.P7STAGE == 24 and a == 0:
                        self.dbg_dump("qfb_dbg", qfb[:], [128, D], BF16)
                        return

                    def _(t, a=a):
                        t.wait_ge(kb.V.sem, kb.V.n)
                        if st["attn_done"]:
                            t.wait_ge(kb.V.sem, st["attn_done"])
                        ins = None
                        for c in range(8):
                            ins = t.transpose(p_tr[:, c * 128:(c + 1) * 128],
                                              qfb[:, c * 128:(c + 1) * 128],
                                              kb.ident_b_sb[:])
                        kb.T.tag(ins)
                    self.on("tensor", _)

                    def _(v, a=a):
                        v.wait_ge(kb.T.sem, kb.T.n)
                        t0 = kb.V.tag(v.tensor_copy(qT[:], p_tr[:]))
                        v.wait_ge(kb.V.sem, t0)
                    self.on("vector", _)
                    if self.P7STAGE == 26 and a == 0:
                        self.dbg_dump("qT_dbg26", qT[:], [128, 8 * 128], BF16)
                        return

                    def _(t, a=a):
                        t.wait_ge(kb.V.sem, kb.V.n)
                        ins = None
                        for c in range(8):
                            ins = t.transpose(p_tr[:, c * 128:(c + 1) * 128],
                                              kfb[:, c * 128:(c + 1) * 128],
                                              kb.ident_b_sb[:])
                        kb.T.tag(ins)
                    self.on("tensor", _)

                    def _(v, a=a):
                        v.wait_ge(kb.T.sem, kb.T.n)
                        t0 = kb.V.tag(v.tensor_copy(
                            kTr[:].rearrange("p (c r w) -> p c r w", c=8, r=3)
                            [:, :, a % 3, :],
                            p_tr[:].rearrange("p (c w) -> p c w", c=8)))
                        v.wait_ge(kb.V.sem, t0)
                    self.on("vector", _)

                    if self.P7STAGE < 3 and a == 3:
                        self.dbg_dump("qT_dbg", qT[:], [128, 8 * 128], BF16)
                        return
                    nw = min(a + 1, 3)
                    msk = m0 if a == 0 else (m1_ if a == 1 else mmain)
                    wlo = max(0, a - 2)
                    W = nw * 128

                    if self.P7STAGE == 27 and a == 0:
                        self.dbg_dump("kT_dbg27", kTr[:], [128, 24 * 128], BF16)
                        return
                    for hg in range(4):
                        def _(t, a=a, hg=hg, nw=nw, wlo=wlo):
                            t.wait_ge(kb.V.sem, kb.V.n)
                            if st["exp"]:
                                t.wait_ge(kb.SC.sem, st["exp"])
                            ins = None
                            for hh in range(4):
                                h = hg * 4 + hh
                                hc, hr = h // 2, (h % 2) * 64
                                for w in range(nw):
                                    kt = wlo + w
                                    ins = t.matmul(
                                        p_att[:, hh * 512 + w * 128:
                                              hh * 512 + (w + 1) * 128],
                                        qT[hr:hr + 64, hc * 128:(hc + 1) * 128],
                                        kTr[hr:hr + 64,
                                            hc * 3 * 128 + (kt % 3) * 128:
                                            hc * 3 * 128 + (kt % 3) * 128 + 128],
                                        start=True, stop=True)
                            kb.T.tag(ins)
                        self.on("tensor", _)

                        def _(v, a=a, hg=hg, W=W, msk=msk):
                            v.wait_ge(kb.T.sem, kb.T.n)
                            t0 = kb.V.tag(v.tensor_tensor(
                                masked[:, 0:4 * W].rearrange(
                                    "p (h w) -> p h w", h=4),
                                p_att[:].rearrange(
                                    "p (h w) -> p h w", h=4)[:, :, 0:W],
                                msk[:, 0:W].unsqueeze(1)
                                .broadcast_to([128, 4, W]),
                                op=ALU.add))
                            v.wait_ge(kb.V.sem, t0)
                            t1 = kb.V.tag(v.tensor_reduce(
                                negmx[:, hg * 4:(hg + 1) * 4],
                                masked[:, 0:4 * W].rearrange(
                                    "p (h w) -> p h w", h=4),
                                axis=AX.X, op=ALU.max, negate=True))
                            v.wait_ge(kb.V.sem, t1)
                            t2 = kb.V.tag(v.tensor_scalar_mul(
                                negmx[:, hg * 4:(hg + 1) * 4],
                                negmx[:, hg * 4:(hg + 1) * 4], 8.0))
                            v.wait_ge(kb.V.sem, t2)
                        self.on("vector", _)

                        def _(sc, a=a, hg=hg, W=W):
                            sc.wait_ge(kb.V.sem, kb.V.n)
                            if st["ptev"]:
                                sc.wait_ge(kb.T.sem, kb.T.n)
                            ins = None
                            for hh in range(4):
                                h = hg * 4 + hh
                                ins = sc.activation(
                                    PT_hg[:, hh * 384:hh * 384 + W],
                                    masked[:, hh * W:(hh + 1) * W],
                                    AF.Exp, scale=8.0,
                                    bias=negmx[:, h:h + 1],
                                    accum_out=den[:, h:h + 1])
                            st["exp"] = kb.SC.tag(ins)
                        self.on("scalar", _)

                        def _(v, a=a, hg=hg):
                            v.wait_ge(kb.SC.sem, kb.SC.n)
                            t0 = kb.V.tag(v.reciprocal(
                                rden[:, hg * 4:(hg + 1) * 4],
                                den[:, hg * 4:(hg + 1) * 4]))
                            v.wait_ge(kb.V.sem, t0)
                        self.on("vector", _)

                        def _(g, a=a, hg=hg, W=W):
                            g.wait_ge(kb.V.sem, kb.V.n)
                            ins = None
                            for hh in range(4):
                                h = hg * 4 + hh
                                ins = g.tensor_scalar_mul(
                                    PT_hg[:, hh * 384:hh * 384 + W],
                                    PT_hg[:, hh * 384:hh * 384 + W],
                                    rden[:, h:h + 1])
                            kb.G.tag(ins)
                        self.on("gpsimd", _)

                        for hh in range(4):
                            h = hg * 4 + hh
                            hc, hr = h // 2, (h % 2) * 64

                            def _(t, a=a, h=h, hh=hh, nw=nw, W=W):
                                t.wait_ge(kb.G.sem, kb.G.n)
                                if st["ptev"]:
                                    t.wait_ge(kb.V.sem, st["ptev"])
                                ins = None
                                for w in range(nw):
                                    ins = t.transpose(
                                        p_tr[:, w * 128:(w + 1) * 128],
                                        PT_hg[:, hh * 384 + w * 128:
                                              hh * 384 + (w + 1) * 128],
                                        kb.ident_b_sb[:])
                                kb.T.tag(ins)
                            self.on("tensor", _)

                            def _(v, a=a, h=h, nw=nw):
                                v.wait_ge(kb.T.sem, kb.T.n)
                                st["ptev"] = kb.V.tag(v.tensor_copy(
                                    ptb[:, 0:nw * 128], p_tr[:, 0:nw * 128]))
                            self.on("vector", _)

                            def _(t, a=a, h=h, nw=nw, wlo=wlo, hr=hr):
                                t.wait_ge(kb.V.sem, kb.V.n)
                                if st["oev"]:
                                    t.wait_ge(kb.V.sem, st["oev"])
                                ins = None
                                for w in range(nw):
                                    kt = wlo + w
                                    ins = t.matmul(
                                        p_O7[hr:hr + 64, 0:128],
                                        vr[:, (kt % 3) * D + h * 64:
                                           (kt % 3) * D + (h + 1) * 64],
                                        ptb[:, w * 128:(w + 1) * 128],
                                        start=(w == 0), stop=(w == nw - 1))
                                kb.T.tag(ins)
                            self.on("tensor", _)

                            def _(v, a=a, h=h, hc=hc, hr=hr):
                                v.wait_ge(kb.T.sem, kb.T.n)
                                st["oev"] = kb.V.tag(v.tensor_copy(
                                    oT_t[hr:hr + 64, hc * 128:(hc + 1) * 128],
                                    p_O7[hr:hr + 64, 0:128]))
                            self.on("vector", _)
                            if self.P7STAGE == 34 and a == 0 and h == 0:
                                self.dbg_dump("oT_dbg34", oT_t[:],
                                              [128, 8 * 128], BF16)
                                return
                    st["attn_done"] = st["oev"]
                    if self.P7STAGE == 23 and a == 3:
                        self.dbg_dump("oT_dbg23", oT_t[:], [128, 8 * 128], BF16)
                        return

                    if self.P7STAGE < 4 and a == 3:
                        self.dbg_dump("oT_dbg", oT_t[:], [128, 8 * 128], BF16)
                        return
                    pf = a % 2

                    def _(sync, a=a, pf=pf, li=li):
                        if st.get("fn_free", [0, 0])[pf]:
                            sync.wait_ge(kb.V.sem, st["fn_free"][pf])
                        else:
                            sync.wait_ge(kb.V.sem, kb.V.n)
                            if dq_fo.n:
                                sync.wait_ge(dq_fo.sem, dq_fo.tok)
                        dq_fb[pf].start(sync, fbuf[pf][:],
                                        (kb.d_xn1 if li == 0 else kb.d_f)
                                        [a * 128:(a + 1) * 128, :])
                    self.on("sync", _)

                    for dh in range(2):
                        def _(t, a=a, dh=dh):
                            t.wait_ge(wo_dq.sem, wo_dq.tok)
                            t.wait_ge(kb.V.sem, kb.V.n)
                            pm = p_wo7 if dh == 0 else p_wo7b
                            ins = None
                            for hp in range(8):
                                ins = t.matmul(
                                    pm,
                                    oT_t[:, hp * 128:(hp + 1) * 128],
                                    wo[:, hp * D + dh * 512:hp * D + (dh + 1) * 512],
                                    start=(hp == 0), stop=(hp == 7))
                            kb.T.tag(ins)
                        self.on("tensor", _)

                        def _(v, a=a, dh=dh, pf=pf):
                            v.wait_ge(kb.T.sem, kb.T.n)
                            v.wait_ge(dq_fb[pf].sem, dq_fb[pf].tok)
                            if dh == 0 and st.get("fo_free", [0, 0])[pf]:
                                v.wait_ge(dq_fo.sem, st["fo_free"][pf])
                            pm = p_wo7 if dh == 0 else p_wo7b
                            t0 = kb.V.tag(v.scalar_tensor_tensor(
                                fnew[pf][:, dh * 512:(dh + 1) * 512],
                                pm, 1.0,
                                fbuf[pf][:, dh * 512:(dh + 1) * 512],
                                op0=ALU.mult, op1=ALU.add))
                            st.setdefault("fn_free", [0, 0])
                            st["fn_free"][pf] = t0
                            v.wait_ge(kb.V.sem, t0)
                        self.on("vector", _)

                    def _(sync, a=a, pf=pf):
                        sync.wait_ge(kb.V.sem, kb.V.n)
                        st.setdefault("fo_free", [0, 0])
                        st["fo_free"][pf] = dq_fo.start(
                            sync, kb.d_f[a * 128:(a + 1) * 128, :], fnew[pf][:])
                    self.on("sync", _)

            actx.close()
            wctx.close()

            if self.P7STAGE < 5:
                self.dbg_dump("fC_dbg", kb.d_f, [S, D])
                return
            transpose_f_pass(kb.d_f)

            ectx = ExitStack()
            w2s = ectx.enter_context(sbt(f"l{li}w2s", [128, 22 * D], BF16))
            stkE = ectx.enter_context(sbt(f"l{li}stk", [2, S], F32))
            rowA = ectx.enter_context(sbt(f"l{li}rwa", [1, S], F32))
            rowB = ectx.enter_context(sbt(f"l{li}rwb", [1, S], F32))
            sqt = ectx.enter_context(sbt(f"l{li}sqt", [128, 512], BF16))
            rstdpp = ectx.enter_context(sbt(f"l{li}rspp", [128, NT], F32))
            gamcol = ectx.enter_context(sbt(f"l{li}gamc", [128, 8], F32))
            gamcolb = ectx.enter_context(sbt(f"l{li}gamcb", [128, 8], BF16))
            betcolb = ectx.enter_context(sbt(f"l{li}betcb", [128, 8], BF16))
            wrows_a = ectx.enter_context(sbt(f"l{li}wra", [2, 512], F32))
            wrows_g = ectx.enter_context(sbt(f"l{li}wrg", [2, 512], F32))
            tmprow = ectx.enter_context(sbt(f"l{li}tmpr", [1, 512], F32))
            ag_sb = ectx.enter_context(sbt(f"l{li}ag", [128, 512], BF16))
            gg_sb = ectx.enter_context(sbt(f"l{li}gg", [128, 512], BF16))
            dq_w2 = self.new_dq("p7w2q")

            def _(g, li=li):
                g.wait_ge(kb.T.sem, kb.T.n)
                g.wait_ge(kb.V.sem, kb.V.n)
                for c in range(22):
                    r0, r1 = c * 128, min((c + 1) * 128, 2730)
                    dq_w2.start(g, w2s[0:r1 - r0, c * D:c * D + D],
                                lw["L1_ffn_W2"][li, r0:r1, :])
            self.on("gpsimd", _)

            lng_row, lng_dq = self.load_row(ectx, f"l{li}lng",
                                            lw["L1_ffn_ln_g"][li, :], D)
            lnb_row, lnb_dq = self.load_row(ectx, f"l{li}lnb",
                                            lw["L1_ffn_ln_b"][li, :], D)
            b2r_, b2_dq = self.load_row(ectx, f"l{li}b2",
                                        lw["L1_ffn_b2"][li, :], D)
            gcol = self.row_to_cols(ectx, f"l{li}gc", lng_row, D, lng_dq, p_bc7)
            bcol = self.row_to_cols(ectx, f"l{li}bc", lnb_row, D, lnb_dq, p_bc7)

            def _(v, li=li):
                v.wait_ge(kb.V.sem, kb.V.n)
                t0 = kb.V.tag(v.tensor_copy(gamcol[:], gcol[:]))
                kb.V.tag(v.tensor_copy(gamcolb[:], gcol[:]))
                t2 = kb.V.tag(v.tensor_copy(betcolb[:], bcol[:]))
                v.wait_ge(kb.V.sem, t2)
            self.on("vector", _)
            gcols_tok = kb.V.n

            stats_ev = [0]
            for tch in range(4):
                def _(t, tch=tch):
                    t.wait_ge(kb.V.sem, kb.V.n)
                    if stats_ev[0]:
                        t.wait_ge(kb.V.sem, stats_ev[0])
                    ins = None
                    for c in range(8):
                        ins = t.matmul(p_r7, onescol_b[:],
                                       xsl(c, tch * 512, 512),
                                       start=(c == 0), stop=(c == 7))
                    kb.T.tag(ins)
                self.on("tensor", _)

                def _(v, tch=tch):
                    v.wait_ge(kb.T.sem, kb.T.n)
                    stats_ev[0] = kb.V.tag(v.tensor_copy(
                        rowA[0:1, tch * 512:(tch + 1) * 512], p_r7))
                self.on("vector", _)

                for c in range(8):
                    def _(sc, tch=tch, c=c):
                        sc.wait_ge(kb.V.sem, kb.V.n)
                        sc.wait_ge(kb.T.sem, kb.T.n)
                        kb.SC.tag(sc.activation(sqt[:], xsl(c, tch * 512, 512),
                                                AF.Square))
                    self.on("scalar", _)

                    def _(t, tch=tch, c=c):
                        t.wait_ge(kb.SC.sem, kb.SC.n)
                        if c == 0:
                            t.wait_ge(kb.V.sem, stats_ev[0])
                        kb.T.tag(t.matmul(p_r7, onescol_b[:], sqt[:],
                                          start=(c == 0), stop=(c == 7)))
                    self.on("tensor", _)

                def _(v, tch=tch):
                    v.wait_ge(kb.T.sem, kb.T.n)
                    stats_ev[0] = kb.V.tag(v.tensor_copy(
                        rowB[0:1, tch * 512:(tch + 1) * 512], p_r7))
                self.on("vector", _)

            def _(v):
                v.wait_ge(kb.V.sem, kb.V.n)
                t0 = kb.V.tag(v.tensor_scalar_mul(rowA[0:1, :],
                                                  rowA[0:1, :], 1.0 / D))
                v.wait_ge(kb.V.sem, t0)
                t1 = kb.V.tag(v.tensor_scalar_mul(rowB[0:1, :],
                                                  rowB[0:1, :], 1.0 / D))
                v.wait_ge(kb.V.sem, t1)
                t2 = kb.V.tag(v.tensor_mul(stkE[0:1, :], rowA[0:1, :],
                                           rowA[0:1, :]))
                v.wait_ge(kb.V.sem, t2)
                t3 = kb.V.tag(v.tensor_sub(rowB[0:1, :], rowB[0:1, :],
                                           stkE[0:1, :]))
                v.wait_ge(kb.V.sem, t3)
            self.on("vector", _)

            def _(sc):
                sc.wait_ge(kb.V.sem, kb.V.n)
                kb.SC.tag(sc.activation(rowB[0:1, :], rowB[0:1, :],
                                        AF.Sqrt, bias=kb.epsb[0:1, 0:1]))
            self.on("scalar", _)

            def _(v, li=li):
                v.wait_ge(kb.SC.sem, kb.SC.n)
                t0 = kb.V.tag(v.tensor_scalar_mul(stkE[0:1, :], rowA[0:1, :],
                                                  -1.0))
                v.wait_ge(kb.V.sem, t0)
            self.on("vector", _)

            def _(sync, li=li):
                sync.wait_ge(kb.V.sem, kb.V.n)
                dq_r1[0].start(sync, stkE[1:2, :], rowB[0:1, :])
            self.on("sync", _)

            rs_ev = [0]
            for i in range(NT):
                def _(t, i=i):
                    t.wait_ge(kb.V.sem, kb.V.n)
                    if rs_ev[0]:
                        t.wait_ge(kb.V.sem, rs_ev[0])
                    kb.T.tag(t.transpose(p_bc7[:, 0:1],
                                         rowB[0:1, i * 128:(i + 1) * 128],
                                         kb.ident_f_sb[0:1, 0:1]))
                self.on("tensor", _)

                def _(v, i=i):
                    v.wait_ge(kb.T.sem, kb.T.n)
                    rs_ev[0] = kb.V.tag(v.tensor_copy(rstdpp[:, i:i + 1],
                                                      p_bc7[:, 0:1]))
                self.on("vector", _)

            def _(v):
                v.wait_ge(kb.V.sem, kb.V.n)
                t0 = kb.V.tag(v.reciprocal(rstdpp[:], rstdpp[:]))
                v.wait_ge(kb.V.sem, t0)
            self.on("vector", _)

            def _(v, li=li):
                v.wait_ge(kb.V.sem, gcols_tok)
                v.wait_ge(kb.T.sem, kb.T.n)
                for c in range(8):
                    t0 = kb.V.tag(v.tensor_scalar_mul(
                        fT[:, c * FT:(c + 1) * FT], fT[:, c * FT:(c + 1) * FT],
                        gamcol[:, c:c + 1]))
                    v.wait_ge(kb.V.sem, t0)
            self.on("vector", _)

            w1b = ectx.enter_context(sbt(f"l{li}w1b", [128, 8 * 1024], BF16))
            agT4 = ectx.enter_context(sbt(f"l{li}agT", [128, 4 * 22 * 128], BF16))
            b1a_row = ectx.enter_context(sbt(f"l{li}b1a", [1, 512], F32))
            b1g_row = ectx.enter_context(sbt(f"l{li}b1g", [1, 512], F32))
            dq_w1 = self.new_dq("p7w1q")
            dq_b1 = self.new_dq("p7b1q")
            w1_free = [0]
            row_ev = [0]
            ag_ev = [0]
            agt_ev = [0]
            f_upd = [0]

            for tg in range(4):
                for j in range(6):
                    jsz = J_SZ[j]

                    def _(g, j=j, jsz=jsz, li=li, tg=tg):
                        if w1_free[0]:
                            g.wait_ge(kb.T.sem, w1_free[0])
                        else:
                            g.wait_ge(kb.T.sem, kb.T.n)
                            g.wait_ge(kb.V.sem, kb.V.n)
                        for ic in range(8):
                            dq_w1.start(
                                g, w1b[:, ic * 1024:ic * 1024 + jsz],
                                lw["L1_ffn_W1"][li, ic * 128:(ic + 1) * 128,
                                                j * 512:j * 512 + jsz])
                            dq_w1.start(
                                g, w1b[:, ic * 1024 + 512:ic * 1024 + 512 + jsz],
                                lw["L1_ffn_W1"][li, ic * 128:(ic + 1) * 128,
                                                2730 + j * 512:2730 + j * 512 + jsz])
                    self.on("gpsimd", _)

                    def _(sync, j=j, jsz=jsz, li=li):
                        sync.wait_ge(kb.T.sem, kb.T.n)
                        dq_b1.start(sync, b1a_row[0:1, 0:jsz],
                                    lw["L1_ffn_b1"][li, j * 512:j * 512 + jsz]
                                    .unsqueeze(0))
                        dq_b1.start(sync, b1g_row[0:1, 0:jsz],
                                    lw["L1_ffn_b1"][li, 2730 + j * 512:
                                                    2730 + j * 512 + jsz]
                                    .unsqueeze(0))
                    self.on("sync", _)

                    for half, ofs, wr, brow in [(0, 0, wrows_a, b1a_row),
                                                (1, 512, wrows_g, b1g_row)]:
                        def _(t, j=j, jsz=jsz, ofs=ofs, half=half):
                            t.wait_ge(dq_w1.sem, dq_w1.tok)
                            t.wait_ge(kb.V.sem, kb.V.n)
                            ins = None
                            for ic in range(8):
                                ins = t.matmul(
                                    p_r7[0:1, 0:jsz], gamcolb[:, ic:ic + 1],
                                    w1b[:, ic * 1024 + ofs:ic * 1024 + ofs + jsz],
                                    start=(ic == 0), stop=(ic == 7))
                            kb.T.tag(ins)
                        self.on("tensor", _)

                        def _(v, j=j, jsz=jsz, wr=wr):
                            v.wait_ge(kb.T.sem, kb.T.n)
                            row_ev[0] = kb.V.tag(v.tensor_copy(
                                wr[0:1, 0:jsz], p_r7[0:1, 0:jsz]))
                        self.on("vector", _)

                        def _(t, j=j, jsz=jsz, ofs=ofs):
                            t.wait_ge(kb.V.sem, kb.V.n)
                            ins = None
                            for ic in range(8):
                                ins = t.matmul(
                                    p_r7[0:1, 0:jsz], betcolb[:, ic:ic + 1],
                                    w1b[:, ic * 1024 + ofs:ic * 1024 + ofs + jsz],
                                    start=(ic == 0), stop=(ic == 7))
                            kb.T.tag(ins)
                        self.on("tensor", _)

                        def _(v, j=j, jsz=jsz, brow=brow):
                            v.wait_ge(kb.T.sem, kb.T.n)
                            v.wait_ge(dq_b1.sem, dq_b1.tok)
                            t0 = kb.V.tag(v.tensor_add(
                                tmprow[0:1, 0:jsz], p_r7[0:1, 0:jsz],
                                brow[0:1, 0:jsz]))
                            v.wait_ge(kb.V.sem, t0)
                        self.on("vector", _)

                        def _(sync, wr=wr, jsz=jsz, half=half):
                            sync.wait_ge(kb.V.sem, kb.V.n)
                            dq_r1[half].start(sync, wr[1:2, 0:jsz],
                                              tmprow[0:1, 0:jsz])
                        self.on("sync", _)

                    for ti4 in range(4):
                        a = tg * 4 + ti4

                        def _(t, a=a, j=j, jsz=jsz):
                            t.wait_ge(dq_r1[0].sem, dq_r1[0].tok)
                            t.wait_ge(dq_r1[1].sem, dq_r1[1].tok)
                            t.wait_ge(kb.V.sem, kb.V.n)
                            if ag_ev[0]:
                                t.wait_ge(kb.SC.sem, kb.SC.n)
                            ins = None
                            for half, ofs, wr in [(0, 0, wrows_a),
                                                  (1, 512, wrows_g)]:
                                pm = p_a7 if half == 0 else p_g7
                                for ic in range(8):
                                    t.matmul(
                                        pm[:, 0:jsz],
                                        xsl(ic, a * 128, 128),
                                        w1b[:, ic * 1024 + ofs:ic * 1024 + ofs + jsz],
                                        start=(ic == 0), stop=False)
                                ins = t.matmul(
                                    pm[:, 0:jsz],
                                    stkE[0:2, a * 128:(a + 1) * 128],
                                    wr[0:2, 0:jsz],
                                    start=False, stop=True)
                            w1_free[0] = kb.T.tag(ins)
                        self.on("tensor", _)

                        def _(sc, a=a, jsz=jsz):
                            sc.wait_ge(kb.T.sem, kb.T.n)
                            kb.SC.tag(sc.activation(
                                gg_sb[:, 0:jsz], p_g7[:, 0:jsz], AF.Gelu,
                                scale=rstdpp[:, a:a + 1]))
                        self.on("scalar", _)

                        def _(v, a=a, jsz=jsz):
                            v.wait_ge(kb.SC.sem, kb.SC.n)
                            if agt_ev[0]:
                                v.wait_ge(kb.T.sem, agt_ev[0])
                            t0 = kb.V.tag(v.tensor_scalar(
                                ag_sb[:, 0:jsz], p_a7[:, 0:jsz],
                                rstdpp[:, a:a + 1], None, op0=ALU.mult))
                            v.wait_ge(kb.V.sem, t0)
                            ag_ev[0] = kb.V.tag(v.tensor_mul(
                                ag_sb[:, 0:jsz], ag_sb[:, 0:jsz],
                                gg_sb[:, 0:jsz]))
                            v.wait_ge(kb.V.sem, ag_ev[0])
                        self.on("vector", _)

                        nsub = (jsz + 127) // 128
                        def _(t, a=a, j=j, jsz=jsz, nsub=nsub, ti4=ti4):
                            t.wait_ge(kb.V.sem, kb.V.n)
                            ins = None
                            for sub in range(nsub):
                                w = min(128, jsz - sub * 128)
                                ins = t.transpose(
                                    p_tr[0:w, sub * 128:sub * 128 + 128],
                                    ag_sb[:, sub * 128:sub * 128 + w],
                                    kb.ident_b_sb[:])
                            agt_ev[0] = kb.T.tag(ins)
                        self.on("tensor", _)

                        def _(v, a=a, j=j, jsz=jsz, nsub=nsub, ti4=ti4):
                            v.wait_ge(kb.T.sem, kb.T.n)
                            for sub in range(nsub):
                                w = min(128, jsz - sub * 128)
                                fc = 4 * j + sub
                                kb.V.tag(v.tensor_copy(
                                    agT4[0:w, (ti4 * 22 + fc) * 128:
                                         (ti4 * 22 + fc) * 128 + 128],
                                    p_tr[0:w, sub * 128:sub * 128 + 128]))
                        self.on("vector", _)

                for ti4 in range(4):
                    a = tg * 4 + ti4

                    def _(sync, a=a, pf=a % 2):
                        sync.wait_ge(kb.V.sem, kb.V.n)
                        sync.wait_ge(dq_fo.sem, dq_fo.tok)
                        dq_fb[pf].start(sync, fbuf[pf][:],
                                        kb.d_f[a * 128:(a + 1) * 128, :])
                    self.on("sync", _)

                    for dh in range(2):
                        pm = p_w2o if dh == 0 else p_w2ob

                        def _(t, a=a, ti4=ti4, dh=dh, pm=pm):
                            t.wait_ge(dq_w2.sem, dq_w2.tok)
                            t.wait_ge(kb.V.sem, kb.V.n)
                            t.wait_ge(b2_dq.sem, b2_dq.tok)
                            ins = None
                            for c in range(22):
                                r = W2C[c]
                                ins = t.matmul(
                                    pm,
                                    agT4[0:r, (ti4 * 22 + c) * 128:
                                         (ti4 * 22 + c) * 128 + 128],
                                    w2s[0:r, c * D + dh * 512:c * D + (dh + 1) * 512],
                                    start=(c == 0), stop=False)
                            ins = t.matmul(pm, kb.ones_row[0:1, 0:128],
                                           b2r_[0:1, dh * 512:(dh + 1) * 512],
                                           start=False, stop=True)
                            kb.T.tag(ins)
                        self.on("tensor", _)

                        def _(v, a=a, dh=dh, pm=pm, pf=a % 2):
                            v.wait_ge(kb.T.sem, kb.T.n)
                            v.wait_ge(dq_fb[pf].sem, dq_fb[pf].tok)
                            if dh == 0 and f_upd[0]:
                                v.wait_ge(dq_fo.sem, f_upd[0])
                            t0 = kb.V.tag(v.scalar_tensor_tensor(
                                fnew[pf][:, dh * 512:(dh + 1) * 512],
                                pm, 1.0,
                                fbuf[pf][:, dh * 512:(dh + 1) * 512],
                                op0=ALU.mult, op1=ALU.add))
                            v.wait_ge(kb.V.sem, t0)
                        self.on("vector", _)

                    def _(sync, a=a, pf=a % 2):
                        sync.wait_ge(kb.V.sem, kb.V.n)
                        f_upd[0] = dq_fo.start(
                            sync, kb.d_f[a * 128:(a + 1) * 128, :], fnew[pf][:])
                    self.on("sync", _)

            ectx.close()
            self.dbg_dump(f"f_l{li}", kb.d_f, [S, D])
        self._p7_fo = dq_fo
        pctx.close()
    def p8(self):
        kb, nc = self, self.nc
        pctx = ExitStack()
        sbt = nc.sbuf_tensor
        xb = [pctx.enter_context(sbt(f"p8x{p}", [128, D], F32)) for p in range(2)]
        nb = [pctx.enter_context(sbt(f"p8n{p}", [128, D], F32)) for p in range(2)]
        yb = [pctx.enter_context(sbt(f"p8y{p}", [128, D], BF16)) for p in range(2)]
        g2 = pctx.enter_context(sbt("p8g", [128, D], BF16))
        fb8 = [pctx.enter_context(sbt(f"p8f{p}", [128, D], F32)) for p in range(2)]
        fdq = [self.new_dq(f"p8fd{p}") for p in range(2)]
        s2 = pctx.enter_context(sbt("p8s2", [128, D], F32))
        ob = [pctx.enter_context(sbt(f"p8o{p}", [128, D], F32)) for p in range(2)]
        ldq = [self.new_dq(f"p8l{p}") for p in range(2)]
        odq = self.new_dq("p8o")
        st = dict(used=[0, 0], ofree=[0, 0])
        bar = self.bar()

        for i in range(NT):
            p = i % 2

            def _(sync, i=i, p=p):
                if st["used"][p]:
                    sync.wait_ge(kb.V.sem, st["used"][p])
                elif i <= 1:
                    self.wait_bar(sync, bar, skip=("dW",))
                    sync.wait_ge(self._p7_fo.sem, self._p7_fo.tok)
                ldq[p].start(sync, xb[p][:], kb.d_x1[i * 128:(i + 1) * 128, :])
                ldq[p].start(sync, nb[p][:], kb.d_xn1[i * 128:(i + 1) * 128, :])
                ldq[p].start(sync, yb[p][:], kb.d_yg[i * 128:(i + 1) * 128, :])
                if st["used"][p]:
                    sync.wait_ge(kb.SC.sem, kb.SC.n)
                fdq[p].start(sync, fb8[p][:], kb.d_f[i * 128:(i + 1) * 128, :])
            self.on("sync", _)

            def _(sc, i=i, p=p):
                sc.wait_ge(kb.V.sem, kb.V.n)
                sc.wait_ge(fdq[p].sem, fdq[p].tok)
                kb.SC.tag(sc.activation(g2[:], fb8[p][:], AF.Silu))
            self.on("scalar", _)

            def _(v, i=i, p=p):
                v.wait_ge(kb.SC.sem, kb.SC.n)
                v.wait_ge(ldq[p].sem, ldq[p].tok)
                t0 = kb.V.tag(v.tensor_add(s2[:], yb[p][:], nb[p][:]))
                v.wait_ge(kb.V.sem, t0)
                t1 = kb.V.tag(v.tensor_mul(s2[:], s2[:], g2[:]))
                v.wait_ge(kb.V.sem, t1)
                if st["ofree"][p]:
                    v.wait_ge(odq.sem, st["ofree"][p])
                t2 = kb.V.tag(v.tensor_add(ob[p][:], xb[p][:], s2[:]))
                st["used"][p] = t2
                v.wait_ge(kb.V.sem, t2)
            self.on("vector", _)

            def _(sync, i=i, p=p):
                sync.wait_ge(kb.V.sem, kb.V.n)
                st["ofree"][p] = odq.start(sync, kb.y[i * 128:(i + 1) * 128, :],
                                           ob[p][:])
            self.on("sync", _)

        def _(sync):
            sync.wait_ge(odq.sem, odq.tok)
        self.on("sync", _)
        pctx.close()

    def finish(self):
        kb = self

        def _(sync):
            sync.wait_ge(kb.T.sem, kb.T.n) if kb.T.n else None
            sync.wait_ge(kb.V.sem, kb.V.n) if kb.V.n else None
            sync.wait_ge(kb.SC.sem, kb.SC.n) if kb.SC.n else None
            sync.wait_ge(kb.G.sem, kb.G.n) if kb.G.n else None
            for name, ap in kb.dbg_outs.items():
                if kb._dbg_src.get(name) is not None:
                    kb.dY.start(sync, ap, kb._dbg_src[name])
            sync.wait_ge(kb.dY.sem, kb.dY.tok)
        self.on("sync", _)
        self.ctx.pop_all()

    def build(self, upto=99):
        self.p0()
        if upto >= 1:
            self.p1()
        if upto >= 2:
            self.p2()
        if upto >= 3:
            self.p3()
            self.p45()
        if upto >= 4:
            self.p6()
        if upto >= 5:
            self.p7(nlayers=min(6, upto - 4) if upto < 11 else 6)
        if upto >= 11:
            self.p8()
        self.finish()
        return self.nc


def make_inputs_for_core(core, inputs):
    x = np.asarray(inputs["L0_attn_x"], dtype=np.float32)
    xfull = np.ascontiguousarray(x.reshape(TFULL, D))
    before = np.zeros((128, NFT), dtype=np.float32)
    before[:, :core * NT] = 1.0
    m = {"x": np.ascontiguousarray(x[core]), "xfull": xfull, "beforet": before}
    for k, v in inputs.items():
        if k == "L0_attn_x":
            continue
        m[k] = np.ascontiguousarray(np.asarray(v, dtype=np.float32))
    return m


def kernel(**inputs):
    kb = KB()
    nc = kb.build(upto=11)
    in_maps = [make_inputs_for_core(c, inputs) for c in range(8)]
    res = run_bass_kernel_spmd(nc, in_maps, core_ids=list(range(8)))
    out = np.stack([np.asarray(res.results[c]["y"]) for c in range(8)], axis=0)
    return np.ascontiguousarray(out.astype(np.float32))
